# version 09
# baseline (speedup 1.0000x reference)
import contextlib

import numpy as np
import concourse.bass as bass
import concourse.mybir as mybir
from concourse.bass_utils import run_bass_kernel_spmd

F32 = mybir.dt.float32
BF16 = mybir.dt.bfloat16
ALU = mybir.AluOpType
ACT = mybir.ActivationFunctionType

NCORES = 8
N = 4096
C = 1024
D = 2048
E = 1024
NSC = 8
TSC = 512
NTB = 4
TB = 128
KC = C // 128
SD = D // 128
ENGS = ("tensor", "vector", "scalar", "sync", "gpsimd")


class Sched:

    _global_reg = {}

    def __init__(self, nc, tag):
        self.nc = nc
        self.tag = tag
        reg = Sched._global_reg.setdefault(id(nc), {"sems": {}, "counts": {}})
        self.sems = reg["sems"]
        self.counts = reg["counts"]
        self.q = {e: [] for e in ENGS}

    def op(self, eng, emit, deps=(), sem=None, amt=1, otok=False):
        deps = [d for d in deps if d is not None]
        for d in deps:
            assert d[0] is not None or d[2] == eng, (eng, d)
        idx = len(self.q[eng])
        tokv = None
        inc = None
        if sem is not None:
            name = sem if sem.startswith("q") else self.tag + "_" + sem
            if name not in self.sems:
                self.sems[name] = self.nc.alloc_semaphore(name)
                self.counts[name] = 0
            self.counts[name] += amt
            inc = (name, amt)
            tokv = (name, self.counts[name], eng, idx)
        elif otok:
            if eng in ("vector", "scalar"):
                return self.op(eng, emit, deps, sem="v" if eng == "vector" else "s")
            tokv = (None, 0, eng, idx)
        if eng == "tensor":
            assert emit is not None, "no nop on tensor engine"
        self.q[eng].append((deps, emit, inc, tokv))
        return tokv

    DMA_POOL = 24

    def dma(self, eng, out, in_, deps=(), sem=None, transpose=False):
        idx = self.counts.get("_dmaidx_" + eng, 0)
        self.counts["_dmaidx_" + eng] = idx + 1
        name = "q%s%02d" % (eng[0], idx % self.DMA_POOL)

        def emit(e, out=out, in_=in_, transpose=transpose):
            return e.dma_start(out=out, in_=in_, transpose=transpose)
        return self.op(eng, emit, deps, sem=name, amt=16)

    def emit_into(self, block):
        for eng in ENGS:
            ops = self.q[eng]
            if not ops:
                continue

            def body(e, eng=eng, ops=ops):
                ensured = {}
                for i, (deps, emitfn, inc, tokv) in enumerate(ops):
                    for (s, v, peng, pidx) in deps:
                        if s is None:
                            assert peng == eng == "tensor", (eng, peng)
                            continue
                        if peng == eng and eng == "tensor":
                            continue
                        if ensured.get(s, 0) >= v:
                            continue
                        e.wait_ge(self.sems[s], v)
                        ensured[s] = v
                    inst = emitfn(e) if emitfn is not None else None
                    if inc is not None:
                        s, amt = inc
                        if inst is None:
                            inst = e.engine_nop() if eng == "vector" else e.nop()
                        inst.then_inc(self.sems[s], amt)

            getattr(block, eng)(body)


SCHEDS = []


def build_nc(debug=False):
    nc = bass.Bass("TRN2", target_bir_lowering=False)

    x_h = nc.dram_tensor("x", [N, C], F32, kind="ExternalInput")
    xT_h = nc.dram_tensor("xT", [C, N], F32, kind="ExternalInput")
    Wq_h = nc.dram_tensor("Wq", [C, D], F32, kind="ExternalInput")
    Wk_h = nc.dram_tensor("Wk", [C, D], F32, kind="ExternalInput")
    WvT_h = nc.dram_tensor("WvT", [D, C], F32, kind="ExternalInput")
    Wo_h = nc.dram_tensor("Wo", [D, E], F32, kind="ExternalInput")
    bqc_h = nc.dram_tensor("bqc", [128, SD], F32, kind="ExternalInput")
    bkc_h = nc.dram_tensor("bkc", [128, SD], F32, kind="ExternalInput")
    bvc_h = nc.dram_tensor("bvc", [128, SD], F32, kind="ExternalInput")
    bo_h = nc.dram_tensor("bo_rep", [128, C], F32, kind="ExternalInput")
    WrT_h = nc.dram_tensor("WrT", [C, C], F32, kind="ExternalInput")
    WkTf_h = nc.dram_tensor("WkTf", [C, C], F32, kind="ExternalInput")
    WvTf_h = nc.dram_tensor("WvTf", [C, C], F32, kind="ExternalInput")
    mur_h = nc.dram_tensor("muc_r", [128, KC], F32, kind="ExternalInput")
    muk_h = nc.dram_tensor("muc_k", [128, KC], F32, kind="ExternalInput")
    triu_h = nc.dram_tensor("triu", [128, 128], F32, kind="ExternalInput")
    ident_h = nc.dram_tensor("ident", [128, 128], F32, kind="ExternalInput")
    out_h = nc.dram_tensor("out", [N, C], F32, kind="ExternalOutput")

    QTd = nc.dram_tensor("QTd", [NSC * D, TSC], BF16, kind="Internal")
    KTd = nc.dram_tensor("KTd", [NSC * D, TSC], BF16, kind="Internal")
    VWd = nc.dram_tensor("VWd", [N, E], BF16, kind="Internal")
    Ybf = nc.dram_tensor("Ybf", [N, C], BF16, kind="Internal")
    Yf = nc.dram_tensor("Yf", [N, C], F32, kind="Internal")
    dST1d = nc.dram_tensor("dST1d", [NSC * 128, 32], F32, kind="Internal")

    stP = contextlib.ExitStack()
    st0 = contextlib.ExitStack()
    stA = contextlib.ExitStack()
    stB = contextlib.ExitStack()
    stC = contextlib.ExitStack()

    def mksb(stack, name, shape, dtype):
        return stack.enter_context(nc.sbuf_tensor(name, shape, dtype))

    def mkps(stack, name, shape):
        return stack.enter_context(nc.psum_tensor(name, shape, F32))

    wq_sb = mksb(stP, "wq_sb", [128, KC * D], BF16)
    wk_sb = mksb(stP, "wk_sb", [128, KC * D], BF16)
    wvwo_sb = mksb(stP, "wvwo_sb", [128, KC * E], BF16)
    bq_sb = mksb(stP, "bq_sb", [128, SD], F32)
    bk_sb = mksb(stP, "bk_sb", [128, SD], F32)
    bvWo_rep = mksb(stP, "bvWo_rep", [128, E], F32)
    wvt_sb = mksb(st0, "wvt_sb", [128, SD * C], BF16)
    wo_sb = mksb(st0, "wo_sb", [128, SD * E], BF16)
    bvc_bf = mksb(st0, "bvc_bf", [128, SD], BF16)
    row_bf = mksb(st0, "row_bf", [1, E], BF16)
    ones_row = mksb(st0, "ones_row", [1, 128], BF16)
    pw = mkps(st0, "pw", [128, 2 * E])

    s0 = Sched(nc, "p0"); SCHEDS.append(s0)
    t_wv = s0.dma("gpsimd",
                  wvt_sb[:].rearrange("p (k c) -> p k c", k=SD),
                  WvT_h.ap().rearrange("(k p) c -> p k c", p=128), sem="dg")
    t_wo = s0.dma("gpsimd",
                  wo_sb[:].rearrange("p (k e) -> p k e", k=SD),
                  Wo_h.ap().rearrange("(k p) e -> p k e", p=128), sem="dg")
    t_bv = s0.dma("gpsimd", bvc_bf[:], bvc_h.ap(), sem="dg")
    s0.dma("gpsimd",
           wq_sb[:].rearrange("p (k d) -> p k d", k=KC),
           Wq_h.ap().rearrange("(k p) d -> p k d", p=128), sem="dg")
    s0.dma("gpsimd",
           wk_sb[:].rearrange("p (k d) -> p k d", k=KC),
           Wk_h.ap().rearrange("(k p) d -> p k d", p=128), sem="dg")
    s0.dma("sync", bq_sb[:], bqc_h.ap(), sem="ds")
    s0.dma("sync", bk_sb[:], bkc_h.ap(), sem="ds")
    t_o1 = s0.op("vector", lambda e: e.memset(ones_row[:], 1.0), sem="v")
    s0.op("scalar", lambda e: (e.preload_act_table(ACT.Exp), e.drain())[-1])

    wvwo_toks = []
    for cs in range(KC):
        reg = (cs % 2) * E
        mm_tok = None
        for k2 in range(SD):
            for h in range(2):
                lhs = wvt_sb[:, k2 * C + cs * 128:k2 * C + (cs + 1) * 128]
                rhs = wo_sb[:, k2 * E + h * 512:k2 * E + (h + 1) * 512]
                outp = pw[:, reg + h * 512:reg + (h + 1) * 512]
                deps = [t_wv, t_wo]
                if cs >= 2:
                    deps.append(wvwo_toks[cs - 2])
                def em(e, o=outp, l=lhs, r=rhs, st=(k2 == 0), sp=(k2 == SD - 1)):
                    return e.matmul(o, l, r, start=st, stop=sp)
                mm_tok = s0.op("tensor", em, deps,
                               sem="t" if (k2 == SD - 1 and h == 1) else None)
        def ev(e, o=wvwo_sb[:, cs * E:(cs + 1) * E], i=pw[:, reg:reg + E]):
            return e.tensor_copy(o, i)
        wvwo_toks.append(s0.op("vector", ev, [mm_tok], sem="v"))

    mm_tok = None
    for k2 in range(SD):
        for h in range(2):
            lhs = bvc_bf[:, k2:k2 + 1]
            rhs = wo_sb[:, k2 * E + h * 512:k2 * E + (h + 1) * 512]
            outp = pw[0:1, h * 512:(h + 1) * 512]
            def em(e, o=outp, l=lhs, r=rhs, st=(k2 == 0), sp=(k2 == SD - 1)):
                return e.matmul(o, l, r, start=st, stop=sp)
            mm_tok = s0.op("tensor", em, [t_bv, wvwo_toks[-2]],
                           sem="t" if (k2 == SD - 1 and h == 1) else None)
    t_row = s0.op("vector", lambda e: e.tensor_copy(row_bf[:], pw[0:1, 0:E]),
                  [mm_tok], sem="v")
    bc_tok = None
    for h in range(2):
        def em(e, o=pw[:, E + h * 512:E + (h + 1) * 512],
               l=ones_row[:], r=row_bf[0:1, h * 512:(h + 1) * 512]):
            return e.matmul(o, l, r, start=True, stop=True)
        bc_tok = s0.op("tensor", em, [t_row, t_o1, wvwo_toks[-1]],
                       sem="t" if h == 1 else None)
    s0.op("vector", lambda e: e.tensor_copy(bvWo_rep[:], pw[:, E:2 * E]),
          [bc_tok], sem="v")

    with nc.named_scope("P0"), nc.Block() as blk0:
        s0.emit_into(blk0)
    st0.close()

    xt_sb = mksb(stA, "xt_sb", [128, 2 * KC * TSC], BF16)
    e0_sb = mksb(stA, "e0_sb", [128, 2 * 512], BF16)
    rr_sb = mksb(stA, "rr_sb", [128, 2 * 512], BF16)
    qt_out = mksb(stA, "qt_out", [128, 2 * SD * 512], BF16)
    kt_out = mksb(stA, "kt_out", [128, 2 * SD * 512], BF16)
    vw_out = mksb(stA, "vw_out", [128, 2 * NTB * E], BF16)
    pa = mkps(stA, "pa", [128, 2048])
    pv = mkps(stA, "pv", [128, 2048])

    sA = Sched(nc, "pA"); SCHEDS.append(sA)
    xt_tok = [None, None]
    xt_free = [None, None]
    st_tok = {"q": [None, None], "k": [None, None], "v": [None, None]}
    bank_free = [None] * 4
    reg_free = [None, None]
    phi_hist = [None, None]

    for sc in range(NSC):
        slot = sc % 2
        xin = xT_h.ap()[:, sc * TSC:(sc + 1) * TSC]
        xt_tok[slot] = sA.dma(
            "gpsimd",
            xt_sb[:, slot * KC * TSC:(slot + 1) * KC * TSC].rearrange(
                "p (k t) -> p k t", k=KC),
            xin.rearrange("(k p) t -> p k t", p=128),
            deps=[xt_free[slot]], sem="dg")
        xslot = slot * KC * TSC

        last_phi = {"q": None, "k": None}
        tile_idx = 0
        for proj, w_sb, b_sb, o_sb in (("q", wq_sb, bq_sb, qt_out),
                                       ("k", wk_sb, bk_sb, kt_out)):
            for s in range(SD):
                bank = tile_idx % 4
                mm_tok = None
                for k in range(KC):
                    lhs = w_sb[:, k * D + s * 128:k * D + (s + 1) * 128]
                    rhs = xt_sb[:, xslot + k * TSC:xslot + (k + 1) * TSC]
                    outp = pa[:, bank * 512:(bank + 1) * 512]
                    deps = []
                    if k == 0:
                        deps = [xt_tok[slot], bank_free[bank]]
                    def em(e, o=outp, l=lhs, r=rhs, st=(k == 0), sp=(k == KC - 1)):
                        return e.matmul(o, l, r, start=st, stop=sp)
                    mm_tok = sA.op("tensor", em, deps,
                                   sem="t" if k == KC - 1 else None)
                es = tile_idx % 2
                psl = pa[:, bank * 512:(bank + 1) * 512]
                def eme(e, o=e0_sb[:, es * 512:(es + 1) * 512], i=psl,
                        b=b_sb[:, s:s + 1]):
                    return e.activation(o, i, ACT.Exp, bias=b, scale=1.0)
                e0_tok = sA.op("scalar", eme, [mm_tok, phi_hist[es]], sem="s")
                def emr(e, o=rr_sb[:, es * 512:(es + 1) * 512], i=psl,
                        b=b_sb[:, s:s + 1]):
                    return e.tensor_scalar(o, i, b, 0.0, ALU.add, ALU.max)
                r_tok = sA.op("vector", emr, [e0_tok], sem="v")
                bank_free[bank] = r_tok
                def emp(e, o=o_sb[:, slot * SD * 512 + s * 512:
                                  slot * SD * 512 + (s + 1) * 512],
                        a=e0_sb[:, es * 512:(es + 1) * 512],
                        b=rr_sb[:, es * 512:(es + 1) * 512]):
                    return e.scalar_tensor_tensor(o, a, 1.0, b, ALU.min, ALU.add)
                phi_tok = sA.op("vector", emp,
                                [e0_tok, r_tok, st_tok[proj][slot]], sem="v")
                phi_hist[es] = phi_tok
                last_phi[proj] = phi_tok
                tile_idx += 1

        last_vw = None
        vmm_last = None
        for tb in range(NTB):
            reg = (tb % 2) * E
            mm_tok = None
            for k in range(KC):
                for h in range(2):
                    lhs = xt_sb[:, xslot + k * TSC + tb * 128:
                                xslot + k * TSC + (tb + 1) * 128]
                    rhs = wvwo_sb[:, k * E + h * 512:k * E + (h + 1) * 512]
                    outp = pv[:, reg + h * 512:reg + (h + 1) * 512]
                    deps = []
                    if k == 0 and h == 0:
                        deps = [xt_tok[slot], reg_free[tb % 2]]
                    def em(e, o=outp, l=lhs, r=rhs, st=(k == 0), sp=(k == KC - 1)):
                        return e.matmul(o, l, r, start=st, stop=sp)
                    mm_tok = sA.op("tensor", em, deps,
                                   sem="t" if (k == KC - 1 and h == 1) else None)
            vmm_last = mm_tok
            def emv(e, o=vw_out[:, slot * NTB * E + tb * E:
                               slot * NTB * E + (tb + 1) * E],
                    i=pv[:, reg:reg + E]):
                return e.tensor_tensor(o, i, bvWo_rep[:], ALU.add)
            last_vw = sA.op("vector", emv, [mm_tok, st_tok["v"][slot]], sem="v")
            reg_free[tb % 2] = last_vw
        xt_free[slot] = vmm_last

        st_tok["q"][slot] = sA.dma(
            "sync",
            QTd.ap()[sc * D:(sc + 1) * D, :].rearrange("(s p) t -> p s t", p=128),
            qt_out[:, slot * SD * 512:(slot + 1) * SD * 512].rearrange(
                "p (s t) -> p s t", s=SD),
            deps=[last_phi["q"]], sem="ds")
        st_tok["k"][slot] = sA.dma(
            "sync",
            KTd.ap()[sc * D:(sc + 1) * D, :].rearrange("(s p) t -> p s t", p=128),
            kt_out[:, slot * SD * 512:(slot + 1) * SD * 512].rearrange(
                "p (s t) -> p s t", s=SD),
            deps=[last_phi["k"]], sem="ds")
        st_tok["v"][slot] = sA.dma(
            "sync",
            VWd.ap()[sc * TSC:(sc + 1) * TSC, :].rearrange(
                "(tb p) e -> p tb e", p=128),
            vw_out[:, slot * NTB * E:(slot + 1) * NTB * E].rearrange(
                "p (tb e) -> p tb e", tb=NTB),
            deps=[last_vw], sem="ds")

    sA.op("sync", None, deps=[st_tok["q"][0], st_tok["q"][1],
                              st_tok["k"][0], st_tok["k"][1],
                              st_tok["v"][0], st_tok["v"][1]])
    with nc.named_scope("PA"), nc.Block() as blkA:
        sA.emit_into(blkA)
    stA.close()
    stP.close()

    sp_sb = mksb(stB, "sp_sb", [128, SD * E], F32)
    sh_sb = mksb(stB, "sh_sb", [128, SD * E], BF16)
    z_sb = mksb(stB, "z_sb", [128, SD], F32)
    zb_sb = mksb(stB, "zb_sb", [128, SD], BF16)
    kt_rb = mksb(stB, "kt_rb", [128, 4 * D], BF16)
    ko_rb = mksb(stB, "ko_rb", [128, 6 * D], BF16)
    vw_rb = mksb(stB, "vw_rb", [128, 4 * E], BF16)
    qt_rb = mksb(stB, "qt_rb", [128, 4 * D], BF16)
    x_rb = mksb(stB, "x_rb", [128, 4 * C], F32)
    xbo_sb = mksb(stB, "xbo_sb", [128, 2 * C], F32)
    r1_sb = mksb(stB, "r1_sb", [128, NTB * C], F32)
    scm_sb = mksb(stB, "scm_sb", [128, 4 * 128], BF16)
    triu_sb = mksb(stB, "triu_sb", [128, 128], F32)
    bo_sb = mksb(stB, "bo_sb", [128, C], F32)
    onec_sb = mksb(stB, "onec_sb", [128, 1], BF16)
    st1_sb = mksb(stB, "st1_sb", [128, 32], F32)
    pn0 = mkps(stB, "pn0", [128, 1536])
    pn1 = mkps(stB, "pn1", [128, 1536])
    pst = mkps(stB, "pst", [128, 1024])

    SUM, SSQ, MNEG, VPE, SR, RSTD, DENT, DENR = (0, 4, 8, 12, 16, 20, 24, 28)

    sB = Sched(nc, "pB"); SCHEDS.append(sB)
    t_triu = sB.dma("sync", triu_sb[:], triu_h.ap(), sem="ds")
    t_bo = sB.dma("sync", bo_sb[:], bo_h.ap(), sem="ds")
    t_one = sB.op("vector", lambda e: e.memset(onec_sb[:], 1.0), sem="v")
    sB.op("scalar", lambda e: (e.preload_act_table(ACT.Sqrt), e.drain())[-1])

    kt_tok = [None] * 4
    kt_free = [None] * 4
    ko_tok = [None] * 6
    ko_free = [None] * 6
    vw_tok = [None] * 4
    vw_free = [None] * 4
    qt_tok = [None] * 4
    qt_free = [None] * 4
    x_tok = [None] * 4
    x_free = [None] * 4
    sh_tok = None
    zb_tok = None
    sc_free = [None, None]
    st_half_free = [None, None]
    nreg_free = [None, None]
    y_store_tok = None
    ybf_store_tok = None
    prev_num_tok = None
    prev_sq = [None] * NTB
    sq_slot_hist = [None, None]
    cast_toks = [None] * SD
    cast_prev = [None] * SD

    def g_of(sc, tb):
        return sc * NTB + tb

    for sc in range(NSC):
        for tb in range(NTB):
            g = g_of(sc, tb)
            k4, q3 = g % 4, g % 4
            x6 = g % 4
            qt_tok[q3] = sB.dma(
                "sync",
                qt_rb[:, q3 * D:(q3 + 1) * D].rearrange("p (s t) -> p s t", s=SD),
                QTd.ap()[sc * D:(sc + 1) * D,
                         tb * 128:(tb + 1) * 128].rearrange(
                             "(s p) t -> p s t", p=128),
                deps=[qt_free[q3]], sem="ds")
            kt_tok[k4] = sB.dma(
                "sync",
                kt_rb[:, k4 * D:(k4 + 1) * D].rearrange("p (s t) -> p s t", s=SD),
                KTd.ap()[sc * D:(sc + 1) * D,
                         tb * 128:(tb + 1) * 128].rearrange(
                             "(s p) t -> p s t", p=128),
                deps=[kt_free[k4]], sem="ds")
            x_tok[x6] = sB.dma(
                "gpsimd", x_rb[:, x6 * C:(x6 + 1) * C],
                x_h.ap()[g * 128:(g + 1) * 128, :],
                deps=[x_free[x6]], sem="dg")
            vw_tok[k4] = sB.dma(
                "gpsimd", vw_rb[:, k4 * E:(k4 + 1) * E],
                VWd.ap()[g * 128:(g + 1) * 128, :],
                deps=[vw_free[k4]], sem="dg")
        for tb in range(NTB):
            g = g_of(sc, tb)
            k8 = g % 6
            ko_tok[k8] = sB.dma(
                "sync", ko_rb[:, k8 * D:(k8 + 1) * D],
                KTd.ap()[sc * D:(sc + 1) * D, tb * 128:(tb + 1) * 128],
                deps=[ko_free[k8]], sem="ds", transpose=True)

        norm_toks = [None] * NTB
        sq_toks = [None] * NTB
        pn0_free_for_z = None
        sc_evac = []
        evac_j_last = [None] * NTB
        last_num_tok = None
        for tb in range(NTB):
            g = g_of(sc, tb)
            q3 = g % 4
            x6 = g % 4
            nreg = pn0 if tb % 2 == 0 else pn1
            qsl = qt_rb[:, q3 * D:(q3 + 1) * D]

            xb = tb % 2
            def emx(e, o=xbo_sb[:, xb * C:(xb + 1) * C],
                    i=x_rb[:, x6 * C:(x6 + 1) * C]):
                return e.tensor_tensor(o, i, bo_sb[:], ALU.add)
            xbo_tok = sB.op("vector", emx, [x_tok[x6], t_bo, sq_slot_hist[xb]], sem="v")
            x_free[x6] = xbo_tok

            sc_evac = []
            for j in range(tb + 1):
                j5 = g_of(sc, j) % 4
                par = j % 2
                mm_tok = None
                for k in range(SD):
                    lhs = kt_rb[:, j5 * D + k * 128:j5 * D + (k + 1) * 128]
                    rhs = qsl[:, k * 128:(k + 1) * 128]
                    outp = pst[:, par * 512:par * 512 + 128]
                    deps = []
                    if k == 0:
                        deps = [kt_tok[j5], qt_tok[q3], sc_free[par]]
                    def em(e, o=outp, l=lhs, r=rhs, st=(k == 0), sp=(k == SD - 1)):
                        return e.matmul(o, l, r, start=st, stop=sp)
                    mm_tok = sB.op("tensor", em, deps,
                                   sem="t" if k == SD - 1 else None)
                if j == tb:
                    def eme(e, o=scm_sb[:, j * 128:(j + 1) * 128],
                            i=pst[:, par * 512:par * 512 + 128]):
                        return e.tensor_tensor(o, i, triu_sb[:], ALU.mult)
                    ev_tok = sB.op("vector", eme, [mm_tok, t_triu, prev_num_tok],
                                   sem="v")
                else:
                    def eme(e, o=scm_sb[:, j * 128:(j + 1) * 128],
                            i=pst[:, par * 512:par * 512 + 128]):
                        return e.activation(o, i, ACT.Copy)
                    ev_tok = sB.op("scalar", eme, [mm_tok, prev_num_tok],
                                   sem="s")
                sc_free[par] = ev_tok
                sc_evac.append(ev_tok)
                evac_j_last[j] = ev_tok

            started = [False, False, False]
            nmm_tok = None

            def nm(rgn, lhs, rhs, extra=(), last=False):
                nonlocal nmm_tok
                lo = (0, 512, 1024)[rgn]
                hi = (512, 1024, 1025)[rgn]
                st = not started[rgn]
                started[rgn] = True
                deps = [d for d in extra if d is not None]
                def em(e, o=nreg[:, lo:hi], l=lhs, r=rhs, st=st, sp=last):
                    return e.matmul(o, l, r, start=st, stop=sp)
                nmm_tok = sB.op("tensor", em, deps, sem="t" if last else None)

            only_intra = (sc == 0)
            for j in range(tb + 1):
                j5 = g_of(sc, j) % 4
                lhs = scm_sb[:, j * 128:(j + 1) * 128]
                isl = (j == tb) and only_intra
                nm(0, lhs, vw_rb[:, j5 * E:j5 * E + 512],
                   extra=[sc_evac[j], vw_tok[j5],
                          nreg_free[tb % 2] if j == 0 else None], last=isl)
                nm(1, lhs, vw_rb[:, j5 * E + 512:(j5 + 1) * E], last=isl)
                nm(2, lhs, onec_sb[:], extra=[t_one], last=isl)
            if sc > 0:
                for s in range(SD):
                    lhs = qsl[:, s * 128:(s + 1) * 128]
                    lasts = (s == SD - 1)
                    nm(0, lhs, sh_sb[:, s * E:s * E + 512],
                       extra=[cast_prev[s]], last=lasts)
                    nm(1, lhs, sh_sb[:, s * E + 512:(s + 1) * E], last=lasts)
                    nm(2, lhs, zb_sb[:, s:s + 1],
                       extra=[zb_tok] if s == 0 else (), last=lasts)
            num_tok = nmm_tok
            last_num_tok = num_tok
            prev_num_tok = num_tok

            def emd(e, o=st1_sb[:, DENT + tb:DENT + tb + 1],
                    i=nreg[:, 1024:1025]):
                return e.tensor_scalar(o, i, 1e-6, None, ALU.add)
            dent_tok = sB.op("vector", emd, [num_tok], otok=True)
            def emr2(e, o=st1_sb[:, DENR + tb:DENR + tb + 1],
                     i=st1_sb[:, DENT + tb:DENT + tb + 1]):
                return e.reciprocal(o, i)
            denr_tok = sB.op("vector", emr2, [dent_tok], otok=True)
            def em1(e, o=r1_sb[:, tb * C:(tb + 1) * C], i=nreg[:, 0:1024],
                    sc1=st1_sb[:, DENR + tb:DENR + tb + 1],
                    i2=xbo_sb[:, xb * C:(xb + 1) * C],
                    ac=st1_sb[:, SUM + tb:SUM + tb + 1]):
                return e.scalar_tensor_tensor(o, i, sc1, i2, ALU.mult, ALU.add,
                                              accum_out=ac)
            r1_tok = sB.op("vector", em1,
                           [denr_tok, xbo_tok, prev_sq[tb],
                            y_store_tok, ybf_store_tok], sem="v")
            nreg_free[tb % 2] = r1_tok
            if tb == 2:
                pn0_free_for_z = r1_tok
            def ems(e, o=xbo_sb[:, xb * C:(xb + 1) * C],
                    i=r1_sb[:, tb * C:(tb + 1) * C],
                    ac=st1_sb[:, SSQ + tb:SSQ + tb + 1]):
                return e.activation(o, i, ACT.Square, accum_out=ac)
            sq_toks[tb] = sB.op("scalar", ems, [r1_tok], sem="s")
            prev_sq[tb] = sq_toks[tb]
            sq_slot_hist[xb] = sq_toks[tb]

        if sc < NSC - 1:
            for s in range(SD):
                half_tok = [None, None]
                smm_tok = None
                for h in range(2):
                    mm_tok = None
                    for tb in range(NTB):
                        k5 = g_of(sc, tb) % 6
                        k4 = g_of(sc, tb) % 4
                        lhs = ko_rb[:, k5 * D + s * 128:k5 * D + (s + 1) * 128]
                        rhs = vw_rb[:, k4 * E + h * 512:k4 * E + (h + 1) * 512]
                        outp = pst[:, h * 512:(h + 1) * 512]
                        deps = [ko_tok[g_of(sc, tb) % 6], vw_tok[k4]]
                        if tb == 0:
                            deps.append(st_half_free[h])
                            if s == 0 and h == 0:
                                deps += [sc_free[0], sc_free[1]]
                        def em(e, o=outp, l=lhs, r=rhs, st=(tb == 0),
                               sp=(tb == NTB - 1)):
                            return e.matmul(o, l, r, start=st, stop=sp)
                        mm_tok = sB.op("tensor", em, deps,
                                       sem="t" if tb == NTB - 1 else None)
                    smm_tok = mm_tok
                    lo = s * E + h * 512
                    if sc == 0:
                        def eme2(e, o=sp_sb[:, lo:lo + 512],
                                 i=pst[:, h * 512:(h + 1) * 512]):
                            return e.tensor_copy(o, i)
                    else:
                        def eme2(e, o=sp_sb[:, lo:lo + 512],
                                 i=pst[:, h * 512:(h + 1) * 512],
                                 b=sp_sb[:, lo:lo + 512]):
                            return e.tensor_tensor(o, i, b, ALU.add)
                    half_tok[h] = sB.op("vector", eme2,
                                        [mm_tok, cast_toks[s]], sem="v")
                    st_half_free[h] = half_tok[h]
                def emc(e, o=sh_sb[:, s * E:(s + 1) * E],
                        i=sp_sb[:, s * E:(s + 1) * E]):
                    return e.activation(o, i, ACT.Copy)
                cast_toks[s] = sB.op("scalar", emc,
                                     [half_tok[1], last_num_tok], sem="s")
            sh_tok = cast_toks[SD - 1]
            cast_prev = list(cast_toks)
            sc_free = [st_half_free[0], st_half_free[1]]

            zmm_tok = None
            for s in range(SD):
                for tb in range(NTB):
                    k5 = g_of(sc, tb) % 6
                    lhs = ko_rb[:, k5 * D + s * 128:k5 * D + (s + 1) * 128]
                    outp = pn0[:, 1040 + s:1041 + s]
                    deps = [ko_tok[g_of(sc, tb) % 6]]
                    if s == 0 and tb == 0:
                        deps += [pn0_free_for_z, t_one]
                    def em(e, o=outp, l=lhs, r=onec_sb[:], st=(tb == 0),
                           sp=(tb == NTB - 1)):
                        return e.matmul(o, l, r, start=st, stop=sp)
                    zmm_tok = sB.op("tensor", em, deps,
                                    sem="t" if (s == SD - 1 and tb == NTB - 1)
                                    else None)
            if sc == 0:
                def emz(e, o=z_sb[:], i=pn0[:, 1040:1040 + SD]):
                    return e.tensor_copy(o, i)
            else:
                def emz(e, o=z_sb[:], i=pn0[:, 1040:1040 + SD], b=z_sb[:]):
                    return e.tensor_tensor(o, i, b, ALU.add)
            zev_tok = sB.op("vector", emz, [zmm_tok, zb_tok], sem="v")
            nreg_free[0] = zev_tok
            def emzc(e, o=zb_sb[:], i=z_sb[:]):
                return e.activation(o, i, ACT.Copy)
            zb_tok = sB.op("scalar", emzc, [zev_tok, last_num_tok], sem="s")

        def emm(e, o=st1_sb[:, MNEG:MNEG + 4], i=st1_sb[:, SUM:SUM + 4]):
            return e.tensor_scalar(o, i, -1.0 / C, None, ALU.mult)
        mneg_tok = sB.op("vector", emm, [sq_toks[-1]], otok=True)
        def emv1(e, o=st1_sb[:, VPE:VPE + 4], i=st1_sb[:, SSQ:SSQ + 4]):
            return e.tensor_scalar(o, i, 1.0 / C, 1e-5, ALU.mult, ALU.add)
        vpe_tok = sB.op("vector", emv1, [sq_toks[-1]], otok=True)
        def emv2(e, o=st1_sb[:, SR:SR + 4], a=st1_sb[:, MNEG:MNEG + 4]):
            return e.tensor_tensor(o, a, a, ALU.mult)
        msq_tok = sB.op("vector", emv2, [mneg_tok], otok=True)
        def emv3(e, o=st1_sb[:, VPE:VPE + 4], a=st1_sb[:, VPE:VPE + 4],
                 b=st1_sb[:, SR:SR + 4]):
            return e.tensor_tensor(o, a, b, ALU.subtract)
        var_tok = sB.op("vector", emv3, [msq_tok, vpe_tok], sem="v")
        def emsq(e, o=st1_sb[:, SR:SR + 4], i=st1_sb[:, VPE:VPE + 4]):
            return e.activation(o, i, ACT.Sqrt)
        sr_tok = sB.op("scalar", emsq, [var_tok], sem="s")
        def emrc(e, o=st1_sb[:, RSTD:RSTD + 4], i=st1_sb[:, SR:SR + 4]):
            return e.reciprocal(o, i)
        rstd_tok = sB.op("vector", emrc, [sr_tok], otok=True)
        for tb in range(NTB):
            def emn(e, o=r1_sb[:, tb * C:(tb + 1) * C],
                    i=r1_sb[:, tb * C:(tb + 1) * C],
                    m=st1_sb[:, MNEG + tb:MNEG + tb + 1],
                    r=st1_sb[:, RSTD + tb:RSTD + tb + 1]):
                return e.tensor_scalar(o, i, m, r, ALU.add, ALU.mult)
            norm_toks[tb] = sB.op("vector", emn, [rstd_tok], sem="v")

        y_store_tok = sB.dma(
            "gpsimd",
            Yf.ap()[sc * TSC:(sc + 1) * TSC, :].rearrange(
                "(tb p) c -> p tb c", p=128),
            r1_sb[:].rearrange("p (tb c) -> p tb c", tb=NTB),
            deps=[norm_toks[3]], sem="dg")
        ybf_store_tok = sB.dma(
            "gpsimd",
            Ybf.ap()[sc * TSC:(sc + 1) * TSC, :].rearrange(
                "(tb p) c -> p tb c", p=128),
            r1_sb[:].rearrange("p (tb c) -> p tb c", tb=NTB),
            deps=[norm_toks[3]], sem="dg")

        sB.dma("gpsimd", dST1d.ap()[sc * 128:(sc + 1) * 128, :], st1_sb[:],
               deps=[norm_toks[3]], sem="dg")

        if sc < NSC - 1:
            for tb in range(NTB):
                g = g_of(sc, tb)
                ko_free[g % 6] = zmm_tok
                vw_free[g % 4] = zmm_tok
                qt_free[g % 4] = norm_toks[tb]
                kt_free[g % 4] = evac_j_last[tb]
        else:
            for tb in range(NTB):
                qt_free[g_of(sc, tb) % 4] = norm_toks[tb]

    sB.op("gpsimd", None, deps=[y_store_tok, ybf_store_tok])
    with nc.named_scope("PB"), nc.Block() as blkB:
        sB.emit_into(blkB)
    stB.close()

    wr_sb = mksb(stC, "wr_sb", [128, KC * C], BF16)
    wkt_sb = mksb(stC, "wkt_sb", [128, KC * C], BF16)
    wvf_sb = mksb(stC, "wvf_sb", [128, KC * C], BF16)
    mur_sb = mksb(stC, "mur_sb", [128, KC], F32)
    muk_sb = mksb(stC, "muk_sb", [128, KC], F32)
    omr_sb = mksb(stC, "omr_sb", [128, KC], F32)
    omk_sb = mksb(stC, "omk_sb", [128, KC], F32)
    yt_sb = mksb(stC, "yt_sb", [128, 2 * KC * 512], BF16)
    ytk_sb = mksb(stC, "ytk_sb", [128, 2 * NTB * C], F32)
    id_sb = mksb(stC, "id_sb", [128, 128], F32)
    xr_sb = mksb(stC, "xr_sb", [128, 2 * KC * 512], BF16)
    xk_sb = mksb(stC, "xk_sb", [128, 2 * KC * 512], BF16)
    tmx_sb = mksb(stC, "tmx_sb", [128, 2 * 512], F32)
    kk_sb = mksb(stC, "kk_sb", [128, 2 * KC * 512], BF16)
    zc_sb = mksb(stC, "zc_sb", [128, 1], BF16)
    krl_sb = mksb(stC, "krl_sb", [128, 2 * 512], BF16)
    rs_sb = mksb(stC, "rs_sb", [128, 2 * C], F32)
    yre_sb = mksb(stC, "yre_sb", [128, 2 * C], F32)
    f_sb = mksb(stC, "f_sb", [128, C], F32)
    r2_sb = mksb(stC, "r2_sb", [128, NTB * C], F32)
    sq2_sb = mksb(stC, "sq2_sb", [128, C], F32)
    st2_sb = mksb(stC, "st2_sb", [128, 32], F32)
    pk = mkps(stC, "pk", [128, 1024])
    pr = mkps(stC, "pr", [128, 1024])
    pvv = mkps(stC, "pvv", [128, 2048])

    SUM2, SSQ2, MNEG2, VPE2, SR2, RSTD2 = (0, 4, 8, 12, 16, 20)

    sC = Sched(nc, "pC"); SCHEDS.append(sC)
    t_wr = sC.dma("gpsimd",
                  wr_sb[:].rearrange("p (k d) -> p k d", k=KC),
                  WrT_h.ap().rearrange("(k p) d -> p k d", p=128), sem="dg")
    t_wkt = sC.dma("gpsimd",
                   wkt_sb[:].rearrange("p (k d) -> p k d", k=KC),
                   WkTf_h.ap().rearrange("(k p) d -> p k d", p=128), sem="dg")
    t_wvf = sC.dma("gpsimd",
                   wvf_sb[:].rearrange("p (k d) -> p k d", k=KC),
                   WvTf_h.ap().rearrange("(k p) d -> p k d", p=128), sem="dg")
    t_id = sC.dma("sync", id_sb[:], ident_h.ap(), sem="ds")
    t_mur = sC.dma("sync", mur_sb[:], mur_h.ap(), sem="ds")
    t_muk = sC.dma("sync", muk_sb[:], muk_h.ap(), sem="ds")
    def emo1(e):
        return e.tensor_scalar(omr_sb[:], mur_sb[:], -1.0, 1.0, ALU.mult, ALU.add)
    t_omr = sC.op("vector", emo1, [t_mur], sem="v")
    def emo2(e):
        return e.tensor_scalar(omk_sb[:], muk_sb[:], -1.0, 1.0, ALU.mult, ALU.add)
    t_omk = sC.op("vector", emo2, [t_muk], sem="v")
    sC.op("scalar", lambda e: (e.preload_act_table(ACT.Sigmoid), e.drain())[-1])

    t_zc = sC.op("vector", lambda e: e.memset(zc_sb[:], 0.0), sem="v")
    yt_tok = [None] * 2
    yt_free = [None] * 2
    ytk_free = [None] * 2
    yt_wfree = [None] * 2
    xrk_free = [None] * 2
    kkb_free = [None] * 2
    yre_tok = [None] * 2
    yre_free = [None] * 2
    o_free = [None] * 2
    pk_free = [None, None]
    out_prev_store = None
    pr_free = None
    pv_free = [None, None]
    f_hist = [None, None]
    prev_sq2 = [None] * NTB
    final_store = None

    def emit_loads_mix(sc):
        slot = sc % 2
        ysl0 = slot * NTB * C
        ld_tok = sC.dma(
            "sync",
            ytk_sb[:, ysl0:ysl0 + NTB * C].rearrange("p (tb c) -> p tb c", tb=NTB),
            Yf.ap()[sc * TSC:(sc + 1) * TSC, :].rearrange(
                "(tb p) c -> p tb c", p=128),
            deps=[ytk_free[slot]], sem="ds")
        ev_tok = None
        for w in range(4):
            tb = w
            reg = (w % 2) * E
            tmm = None
            for si in range(KC):
                def emt_(e, o=pvv[:, reg + si * 128:reg + (si + 1) * 128],
                         i=ytk_sb[:, ysl0 + tb * C + si * 128:
                                  ysl0 + tb * C + (si + 1) * 128]):
                    return e.transpose(o, i, id_sb[:])
                tmm = sC.op("tensor", emt_,
                            [ld_tok, t_id, pv_free[w % 2]] if si == 0 else [],
                            sem="t" if si == KC - 1 else None)
            for si in range(KC):
                def emv_(e, o=yt_sb[:, slot * KC * 512 + si * 512 + tb * 128:
                                    slot * KC * 512 + si * 512 + (tb + 1) * 128],
                         i=pvv[:, reg + si * 128:reg + (si + 1) * 128]):
                    return e.tensor_copy(o, i)
                ev_tok = sC.op("vector", emv_, [tmm, yt_wfree[slot]], sem="v")
            pv_free[w % 2] = ev_tok
        yt_tok[slot] = ev_tok
        ytk_free[slot] = ev_tok
        pslot = 1 - slot

        mix_last = {"r": None, "k": None}
        xof = slot * KC * 512
        for s in range(KC):
            ysl = yt_sb[:, slot * KC * 512 + s * 512:slot * KC * 512 + (s + 1) * 512]
            psh = yt_sb[:, slot * KC * 512 + s * 512:slot * KC * 512 + s * 512 + 511]
            if sc == 0:
                pcol = zc_sb[:]
                pcol_dep = t_zc
            else:
                pcol = yt_sb[:, pslot * KC * 512 + s * 512 + 511:
                             pslot * KC * 512 + s * 512 + 512]
                pcol_dep = yt_tok[pslot]
            for which, xo_sb, mu, om in (("r", xr_sb, mur_sb, omr_sb),
                                         ("k", xk_sb, muk_sb, omk_sb)):
                ts_ = (s % 2)
                def emt(e, o=tmx_sb[:, ts_ * 512 + 1:(ts_ + 1) * 512], i=psh,
                        m=om[:, s:s + 1]):
                    return e.tensor_scalar(o, i, m, None, ALU.mult)
                tm_tok = sC.op("vector", emt,
                               [yt_tok[slot], t_omr, t_omk, xrk_free[slot]],
                               otok=True)
                def emt0(e, o=tmx_sb[:, ts_ * 512:ts_ * 512 + 1], i=pcol,
                         m=om[:, s:s + 1]):
                    return e.tensor_scalar(o, i, m, None, ALU.mult)
                tm0_tok = sC.op("vector", emt0, [pcol_dep], otok=True)
                def emm2(e, o=xo_sb[:, xof + s * 512:xof + (s + 1) * 512], i=ysl,
                         m=mu[:, s:s + 1],
                         b=tmx_sb[:, ts_ * 512:(ts_ + 1) * 512]):
                    return e.scalar_tensor_tensor(o, i, m, b, ALU.mult, ALU.add)
                mix_last[which] = sC.op("vector", emm2,
                                        [tm_tok, tm0_tok, t_mur, t_muk],
                                        sem="v")
        yt_free[pslot] = mix_last["k"]
        yt_wfree[slot] = mix_last["k"]
        return mix_last

    mixes = {0: emit_loads_mix(0)}
    for sc in range(NSC):
        slot = sc % 2
        xof = slot * KC * 512
        if sc + 1 < NSC:
            mixes[sc + 1] = emit_loads_mix(sc + 1)
        mix_last = mixes[sc]

        kk_last = None
        for s in range(KC):
            par = s % 2
            mm_tok = None
            for k in range(KC):
                lhs = wkt_sb[:, k * C + s * 128:k * C + (s + 1) * 128]
                rhs = xk_sb[:, xof + k * 512:xof + (k + 1) * 512]
                outp = pk[:, par * 512:(par + 1) * 512]
                deps = []
                if k == 0:
                    deps = [mix_last["k"], t_wkt, pk_free[par]]
                def em(e, o=outp, l=lhs, r=rhs, st=(k == 0), sp=(k == KC - 1)):
                    return e.matmul(o, l, r, start=st, stop=sp)
                mm_tok = sC.op("tensor", em, deps,
                               sem="t" if k == KC - 1 else None)
            def emrl(e, o=krl_sb[:, par * 512:(par + 1) * 512],
                     i=pk[:, par * 512:(par + 1) * 512]):
                return e.activation(o, i, ACT.Relu)
            rl_tok = sC.op("scalar", emrl, [mm_tok], sem="s")
            pk_free[par] = rl_tok
            def emsq2(e, o=kk_sb[:, xof + s * 512:xof + (s + 1) * 512],
                      i=krl_sb[:, par * 512:(par + 1) * 512]):
                return e.activation(o, i, ACT.Square)
            kk_last = sC.op("scalar", emsq2, [rl_tok, kkb_free[slot]], sem="s")

        sq2_toks = [None] * NTB
        last_v_mm = None
        for tb in range(NTB):
            g = g_of(sc, tb)
            yr2 = tb % 2
            yre_tok[yr2] = sC.dma("gpsimd",
                                  yre_sb[:, yr2 * C:(yr2 + 1) * C],
                                  Yf.ap()[g * 128:(g + 1) * 128, :],
                                  deps=[yre_free[yr2]], sem="dg")
            mm_tok = None
            for k in range(KC):
                for h in range(2):
                    lhs = xr_sb[:, xof + k * 512 + tb * 128:xof + k * 512 + (tb + 1) * 128]
                    rhs = wr_sb[:, k * C + h * 512:k * C + (h + 1) * 512]
                    outp = pr[:, h * 512:(h + 1) * 512]
                    deps = []
                    if k == 0 and h == 0:
                        deps = [mix_last["r"], t_wr, pr_free]
                    def em(e, o=outp, l=lhs, r=rhs, st=(k == 0), sp=(k == KC - 1)):
                        return e.matmul(o, l, r, start=st, stop=sp)
                    mm_tok = sC.op("tensor", em, deps,
                                   sem="t" if (k == KC - 1 and h == 1) else None)
            rsl = tb % 2
            def emsg(e, o=rs_sb[:, rsl * C:(rsl + 1) * C], i=pr[:, 0:1024]):
                return e.activation(o, i, ACT.Sigmoid)
            sg_tok = sC.op("scalar", emsg, [mm_tok, f_hist[rsl]], sem="s")
            pr_free = sg_tok
            vreg = (tb % 2) * E
            mm_tok = None
            for k in range(KC):
                for h in range(2):
                    lhs = kk_sb[:, xof + k * 512 + tb * 128:xof + k * 512 + (tb + 1) * 128]
                    rhs = wvf_sb[:, k * C + h * 512:k * C + (h + 1) * 512]
                    outp = pvv[:, vreg + h * 512:vreg + (h + 1) * 512]
                    deps = []
                    if k == 0 and h == 0:
                        deps = [kk_last, t_wvf, pv_free[tb % 2]]
                    def em(e, o=outp, l=lhs, r=rhs, st=(k == 0), sp=(k == KC - 1)):
                        return e.matmul(o, l, r, start=st, stop=sp)
                    mm_tok = sC.op("tensor", em, deps,
                                   sem="t" if (k == KC - 1 and h == 1) else None)
            last_v_mm = mm_tok
            def emf(e, o=f_sb[:], i=pvv[:, vreg:vreg + E],
                    b=rs_sb[:, rsl * C:(rsl + 1) * C]):
                return e.tensor_tensor(o, i, b, ALU.mult)
            f_tok = sC.op("vector", emf, [mm_tok, sg_tok], sem="v")
            pv_free[tb % 2] = f_tok
            f_hist[rsl] = f_tok
            def em2(e, o=r2_sb[:, tb * C:(tb + 1) * C], i=f_sb[:],
                    b=yre_sb[:, yr2 * C:(yr2 + 1) * C],
                    ac=st2_sb[:, SUM2 + tb:SUM2 + tb + 1]):
                return e.scalar_tensor_tensor(o, i, 1.0, b, ALU.mult, ALU.add,
                                              accum_out=ac)
            r2_tok = sC.op("vector", em2,
                           [f_tok, yre_tok[yr2], prev_sq2[tb], out_prev_store],
                           sem="v")
            yre_free[yr2] = r2_tok
            def ems2(e, o=sq2_sb[:], i=r2_sb[:, tb * C:(tb + 1) * C],
                     ac=st2_sb[:, SSQ2 + tb:SSQ2 + tb + 1]):
                return e.activation(o, i, ACT.Square, accum_out=ac)
            sq2_toks[tb] = sC.op("scalar", ems2, [r2_tok], sem="s")
            prev_sq2[tb] = sq2_toks[tb]
        xrk_free[slot] = last_v_mm
        kkb_free[slot] = last_v_mm

        def emm_(e, o=st2_sb[:, MNEG2:MNEG2 + 4], i=st2_sb[:, SUM2:SUM2 + 4]):
            return e.tensor_scalar(o, i, -1.0 / C, None, ALU.mult)
        mneg_tok = sC.op("vector", emm_, [sq2_toks[-1]], otok=True)
        def emv1_(e, o=st2_sb[:, VPE2:VPE2 + 4], i=st2_sb[:, SSQ2:SSQ2 + 4]):
            return e.tensor_scalar(o, i, 1.0 / C, 1e-5, ALU.mult, ALU.add)
        vpe_tok = sC.op("vector", emv1_, [sq2_toks[-1]], otok=True)
        def emv2_(e, o=st2_sb[:, SR2:SR2 + 4], a=st2_sb[:, MNEG2:MNEG2 + 4]):
            return e.tensor_tensor(o, a, a, ALU.mult)
        msq_tok = sC.op("vector", emv2_, [mneg_tok], otok=True)
        def emv3_(e, o=st2_sb[:, VPE2:VPE2 + 4], a=st2_sb[:, VPE2:VPE2 + 4],
                  b=st2_sb[:, SR2:SR2 + 4]):
            return e.tensor_tensor(o, a, b, ALU.subtract)
        var_tok = sC.op("vector", emv3_, [msq_tok, vpe_tok], sem="v")
        def emsd(e, o=st2_sb[:, SR2:SR2 + 4], i=st2_sb[:, VPE2:VPE2 + 4]):
            return e.tensor_scalar(o, i, 0.5, 0.5, ALU.mult, ALU.add)
        it_tok = sC.op("vector", emsd, [var_tok], otok=True)
        for _ in range(5):
            def emr1_(e, o=st2_sb[:, 24:28], i=st2_sb[:, SR2:SR2 + 4]):
                return e.reciprocal(o, i)
            t1 = sC.op("vector", emr1_, [it_tok], otok=True)
            def emr2_(e, o=st2_sb[:, 28:32], a=st2_sb[:, 24:28],
                      b=st2_sb[:, VPE2:VPE2 + 4]):
                return e.tensor_tensor(o, a, b, ALU.mult)
            t2 = sC.op("vector", emr2_, [t1], otok=True)
            def emr3_(e, o=st2_sb[:, SR2:SR2 + 4], a=st2_sb[:, SR2:SR2 + 4],
                      b=st2_sb[:, 28:32]):
                return e.tensor_tensor(o, a, b, ALU.add)
            t3 = sC.op("vector", emr3_, [t2], otok=True)
            def emr4_(e, o=st2_sb[:, SR2:SR2 + 4], i=st2_sb[:, SR2:SR2 + 4]):
                return e.tensor_scalar(o, i, 0.5, None, ALU.mult)
            it_tok = sC.op("vector", emr4_, [t3], otok=True)
        def emrc_(e, o=st2_sb[:, RSTD2:RSTD2 + 4], i=st2_sb[:, SR2:SR2 + 4]):
            return e.reciprocal(o, i)
        rstd_tok = sC.op("vector", emrc_, [it_tok], otok=True)
        o_norm = None
        for tb in range(NTB):
            def emn_(e, o=r2_sb[:, tb * C:(tb + 1) * C],
                     i=r2_sb[:, tb * C:(tb + 1) * C],
                     m=st2_sb[:, MNEG2 + tb:MNEG2 + tb + 1],
                     r=st2_sb[:, RSTD2 + tb:RSTD2 + tb + 1]):
                return e.tensor_scalar(o, i, m, r, ALU.add, ALU.mult)
            o_norm = sC.op("vector", emn_, [rstd_tok], sem="v")
        final_store = sC.dma(
            "sync",
            out_h.ap()[sc * TSC:(sc + 1) * TSC, :].rearrange(
                "(tb p) c -> p tb c", p=128),
            r2_sb[:].rearrange("p (tb c) -> p tb c", tb=NTB),
            deps=[o_norm], sem="dout")
        o_free[slot] = final_store
        out_prev_store = final_store

    sC.op("sync", None, deps=[final_store])
    sC.op("gpsimd", None, deps=[final_store])

    with nc.named_scope("PC"), nc.Block() as blkC:
        sC.emit_into(blkC)
    stC.close()

    if debug:
        dQT = nc.dram_tensor("dQT", [NSC * D, TSC], BF16, kind="ExternalOutput")
        dKT = nc.dram_tensor("dKT", [NSC * D, TSC], BF16, kind="ExternalOutput")
        dVW = nc.dram_tensor("dVW", [N, E], BF16, kind="ExternalOutput")
        dYf = nc.dram_tensor("dYf", [N, C], F32, kind="ExternalOutput")
        dYb = nc.dram_tensor("dYb", [N, C], BF16, kind="ExternalOutput")
        dST1 = nc.dram_tensor("dST1", [NSC * 128, 32], F32, kind="ExternalOutput")
        sD = Sched(nc, "pD")
        tks = [sD.dma("gpsimd", dQT.ap(), QTd.ap(), sem="dg"),
               sD.dma("gpsimd", dKT.ap(), KTd.ap(), sem="dg"),
               sD.dma("gpsimd", dVW.ap(), VWd.ap(), sem="dg"),
               sD.dma("gpsimd", dYf.ap(), Yf.ap(), sem="dg"),
               sD.dma("gpsimd", dYb.ap(), Ybf.ap(), sem="dg"),
               sD.dma("gpsimd", dST1.ap(), dST1d.ap(), sem="dg")]
        sD.op("gpsimd", None, deps=tks)
        with nc.Block() as blkD:
            sD.emit_into(blkD)

    nc.finalize()
    return nc


_nc_cache = None


def kernel(L0_attn_x, L0_attn_Wq, L0_attn_bq, L0_attn_Wk, L0_attn_bk,
           L0_attn_Wv, L0_attn_bv, L0_attn_Wo, L0_attn_bo,
           L0_ffn_W_r, L0_ffn_W_k, L0_ffn_W_v, L0_ffn_mu_r, L0_ffn_mu_k):
    global _nc_cache
    f = np.float32
    ca = np.ascontiguousarray

    x = np.asarray(L0_attn_x, f)
    shared = dict(
        Wq=ca(np.asarray(L0_attn_Wq, f)),
        Wk=ca(np.asarray(L0_attn_Wk, f)),
        WvT=ca(np.asarray(L0_attn_Wv, f).T),
        Wo=ca(np.asarray(L0_attn_Wo, f)),
        bqc=ca(np.asarray(L0_attn_bq, f).reshape(SD, 128).T),
        bkc=ca(np.asarray(L0_attn_bk, f).reshape(SD, 128).T),
        bvc=ca(np.asarray(L0_attn_bv, f).reshape(SD, 128).T),
        bo_rep=ca(np.broadcast_to(np.asarray(L0_attn_bo, f), (128, C))),
        WrT=ca(np.asarray(L0_ffn_W_r, f).T),
        WkTf=ca(np.asarray(L0_ffn_W_k, f).T),
        WvTf=ca(np.asarray(L0_ffn_W_v, f).T),
        muc_r=ca(np.asarray(L0_ffn_mu_r, f).reshape(KC, 128).T),
        muc_k=ca(np.asarray(L0_ffn_mu_k, f).reshape(KC, 128).T),
        triu=np.triu(np.ones((128, 128), f)),
        ident=np.eye(128, dtype=f),
    )
    in_maps = []
    for b in range(NCORES):
        m = dict(shared)
        m["x"] = ca(x[b])
        m["xT"] = ca(x[b].T)
        in_maps.append(m)

    if _nc_cache is None:
        _nc_cache = build_nc()
    res = run_bass_kernel_spmd(_nc_cache, in_maps, core_ids=list(range(NCORES)))
    return np.stack([r["out"] for r in res.results], axis=0)


# version 10
# speedup vs baseline: 1.0728x; 1.0728x over previous
import contextlib

import numpy as np
import concourse.bass as bass
import concourse.mybir as mybir
from concourse.bass_utils import run_bass_kernel_spmd

F32 = mybir.dt.float32
BF16 = mybir.dt.bfloat16
ALU = mybir.AluOpType
ACT = mybir.ActivationFunctionType

NCORES = 8
N = 4096
C = 1024
D = 2048
E = 1024
NSC = 8
TSC = 512
NTB = 4
TB = 128
KC = C // 128
SD = D // 128
ENGS = ("tensor", "vector", "scalar", "sync", "gpsimd")


class Sched:

    _global_reg = {}

    def __init__(self, nc, tag):
        self.nc = nc
        self.tag = tag
        reg = Sched._global_reg.setdefault(id(nc), {"sems": {}, "counts": {}})
        self.sems = reg["sems"]
        self.counts = reg["counts"]
        self.q = {e: [] for e in ENGS}

    def op(self, eng, emit, deps=(), sem=None, amt=1, otok=False):
        deps = [d for d in deps if d is not None]
        for d in deps:
            assert d[0] is not None or d[2] == eng, (eng, d)
        idx = len(self.q[eng])
        tokv = None
        inc = None
        if sem is not None:
            name = sem if sem.startswith("q") else self.tag + "_" + sem
            if name not in self.sems:
                self.sems[name] = self.nc.alloc_semaphore(name)
                self.counts[name] = 0
            self.counts[name] += amt
            inc = (name, amt)
            tokv = (name, self.counts[name], eng, idx)
        elif otok:
            if eng in ("vector", "scalar"):
                return self.op(eng, emit, deps, sem="v" if eng == "vector" else "s")
            tokv = (None, 0, eng, idx)
        if eng == "tensor":
            assert emit is not None, "no nop on tensor engine"
        self.q[eng].append((deps, emit, inc, tokv))
        return tokv

    DMA_POOL = 24

    def dma(self, eng, out, in_, deps=(), sem=None, transpose=False):
        idx = self.counts.get("_dmaidx_" + eng, 0)
        self.counts["_dmaidx_" + eng] = idx + 1
        name = "q%s%02d" % (eng[0], idx % self.DMA_POOL)

        def emit(e, out=out, in_=in_, transpose=transpose):
            return e.dma_start(out=out, in_=in_, transpose=transpose)
        return self.op(eng, emit, deps, sem=name, amt=16)

    def emit_into(self, block):
        for eng in ENGS:
            ops = self.q[eng]
            if not ops:
                continue

            def body(e, eng=eng, ops=ops):
                ensured = {}
                for i, (deps, emitfn, inc, tokv) in enumerate(ops):
                    for (s, v, peng, pidx) in deps:
                        if s is None:
                            assert peng == eng == "tensor", (eng, peng)
                            continue
                        if peng == eng and eng == "tensor":
                            continue
                        if ensured.get(s, 0) >= v:
                            continue
                        e.wait_ge(self.sems[s], v)
                        ensured[s] = v
                    inst = emitfn(e) if emitfn is not None else None
                    if inc is not None:
                        s, amt = inc
                        if inst is None:
                            inst = e.engine_nop() if eng == "vector" else e.nop()
                        inst.then_inc(self.sems[s], amt)

            getattr(block, eng)(body)


SCHEDS = []


def build_nc(debug=False):
    nc = bass.Bass("TRN2", target_bir_lowering=False)

    x_h = nc.dram_tensor("x", [N, C], F32, kind="ExternalInput")
    xT_h = nc.dram_tensor("xT", [C, N], F32, kind="ExternalInput")
    Wq_h = nc.dram_tensor("Wq", [C, D], F32, kind="ExternalInput")
    Wk_h = nc.dram_tensor("Wk", [C, D], F32, kind="ExternalInput")
    WvT_h = nc.dram_tensor("WvT", [D, C], F32, kind="ExternalInput")
    Wo_h = nc.dram_tensor("Wo", [D, E], F32, kind="ExternalInput")
    bqc_h = nc.dram_tensor("bqc", [128, SD], F32, kind="ExternalInput")
    bkc_h = nc.dram_tensor("bkc", [128, SD], F32, kind="ExternalInput")
    bvc_h = nc.dram_tensor("bvc", [128, SD], F32, kind="ExternalInput")
    bo_h = nc.dram_tensor("bo_rep", [128, C], F32, kind="ExternalInput")
    WrT_h = nc.dram_tensor("WrT", [C, C], F32, kind="ExternalInput")
    WkTf_h = nc.dram_tensor("WkTf", [C, C], F32, kind="ExternalInput")
    WvTf_h = nc.dram_tensor("WvTf", [C, C], F32, kind="ExternalInput")
    mur_h = nc.dram_tensor("muc_r", [128, KC], F32, kind="ExternalInput")
    muk_h = nc.dram_tensor("muc_k", [128, KC], F32, kind="ExternalInput")
    triu_h = nc.dram_tensor("triu", [128, 128], F32, kind="ExternalInput")
    ident_h = nc.dram_tensor("ident", [128, 128], F32, kind="ExternalInput")
    out_h = nc.dram_tensor("out", [N, C], F32, kind="ExternalOutput")

    QTd = nc.dram_tensor("QTd", [NSC * D, TSC], BF16, kind="Internal")
    KTd = nc.dram_tensor("KTd", [NSC * D, TSC], BF16, kind="Internal")
    VWd = nc.dram_tensor("VWd", [N, E], BF16, kind="Internal")
    Ybf = nc.dram_tensor("Ybf", [N, C], BF16, kind="Internal")
    Yf = nc.dram_tensor("Yf", [N, C], F32, kind="Internal")
    dST1d = nc.dram_tensor("dST1d", [NSC * 128, 32], F32, kind="Internal")

    stP = contextlib.ExitStack()
    st0 = contextlib.ExitStack()
    stA = contextlib.ExitStack()
    stB = contextlib.ExitStack()
    stC = contextlib.ExitStack()

    def mksb(stack, name, shape, dtype):
        return stack.enter_context(nc.sbuf_tensor(name, shape, dtype))

    def mkps(stack, name, shape):
        return stack.enter_context(nc.psum_tensor(name, shape, F32))

    wq_sb = mksb(stP, "wq_sb", [128, KC * D], BF16)
    wk_sb = mksb(stP, "wk_sb", [128, KC * D], BF16)
    wvwo_sb = mksb(stP, "wvwo_sb", [128, KC * E], BF16)
    bq_sb = mksb(stP, "bq_sb", [128, SD], F32)
    bk_sb = mksb(stP, "bk_sb", [128, SD], F32)
    bvWo_rep = mksb(stP, "bvWo_rep", [128, E], F32)
    wvt_sb = mksb(st0, "wvt_sb", [128, SD * C], BF16)
    wo_sb = mksb(st0, "wo_sb", [128, SD * E], BF16)
    bvc_bf = mksb(st0, "bvc_bf", [128, SD], BF16)
    row_bf = mksb(st0, "row_bf", [1, E], BF16)
    ones_row = mksb(st0, "ones_row", [1, 128], BF16)
    pw = mkps(st0, "pw", [128, 2 * E])

    s0 = Sched(nc, "p0"); SCHEDS.append(s0)
    t_wv = s0.dma("gpsimd",
                  wvt_sb[:].rearrange("p (k c) -> p k c", k=SD),
                  WvT_h.ap().rearrange("(k p) c -> p k c", p=128), sem="dg")
    t_wo = s0.dma("gpsimd",
                  wo_sb[:].rearrange("p (k e) -> p k e", k=SD),
                  Wo_h.ap().rearrange("(k p) e -> p k e", p=128), sem="dg")
    t_bv = s0.dma("gpsimd", bvc_bf[:], bvc_h.ap(), sem="dg")
    s0.dma("gpsimd",
           wq_sb[:].rearrange("p (k d) -> p k d", k=KC),
           Wq_h.ap().rearrange("(k p) d -> p k d", p=128), sem="dg")
    s0.dma("gpsimd",
           wk_sb[:].rearrange("p (k d) -> p k d", k=KC),
           Wk_h.ap().rearrange("(k p) d -> p k d", p=128), sem="dg")
    s0.dma("sync", bq_sb[:], bqc_h.ap(), sem="ds")
    s0.dma("sync", bk_sb[:], bkc_h.ap(), sem="ds")
    t_o1 = s0.op("vector", lambda e: e.memset(ones_row[:], 1.0), sem="v")
    s0.op("scalar", lambda e: (e.preload_act_table(ACT.Exp), e.drain())[-1])

    wvwo_toks = []
    for cs in range(KC):
        reg = (cs % 2) * E
        mm_tok = None
        for k2 in range(SD):
            for h in range(2):
                lhs = wvt_sb[:, k2 * C + cs * 128:k2 * C + (cs + 1) * 128]
                rhs = wo_sb[:, k2 * E + h * 512:k2 * E + (h + 1) * 512]
                outp = pw[:, reg + h * 512:reg + (h + 1) * 512]
                deps = [t_wv, t_wo]
                if cs >= 2:
                    deps.append(wvwo_toks[cs - 2])
                def em(e, o=outp, l=lhs, r=rhs, st=(k2 == 0), sp=(k2 == SD - 1)):
                    return e.matmul(o, l, r, start=st, stop=sp)
                mm_tok = s0.op("tensor", em, deps,
                               sem="t" if (k2 == SD - 1 and h == 1) else None)
        def ev(e, o=wvwo_sb[:, cs * E:(cs + 1) * E], i=pw[:, reg:reg + E]):
            return e.tensor_copy(o, i)
        wvwo_toks.append(s0.op("vector", ev, [mm_tok], sem="v"))

    mm_tok = None
    for k2 in range(SD):
        for h in range(2):
            lhs = bvc_bf[:, k2:k2 + 1]
            rhs = wo_sb[:, k2 * E + h * 512:k2 * E + (h + 1) * 512]
            outp = pw[0:1, h * 512:(h + 1) * 512]
            def em(e, o=outp, l=lhs, r=rhs, st=(k2 == 0), sp=(k2 == SD - 1)):
                return e.matmul(o, l, r, start=st, stop=sp)
            mm_tok = s0.op("tensor", em, [t_bv, wvwo_toks[-2]],
                           sem="t" if (k2 == SD - 1 and h == 1) else None)
    t_row = s0.op("vector", lambda e: e.tensor_copy(row_bf[:], pw[0:1, 0:E]),
                  [mm_tok], sem="v")
    bc_tok = None
    for h in range(2):
        def em(e, o=pw[:, E + h * 512:E + (h + 1) * 512],
               l=ones_row[:], r=row_bf[0:1, h * 512:(h + 1) * 512]):
            return e.matmul(o, l, r, start=True, stop=True)
        bc_tok = s0.op("tensor", em, [t_row, t_o1, wvwo_toks[-1]],
                       sem="t" if h == 1 else None)
    s0.op("vector", lambda e: e.tensor_copy(bvWo_rep[:], pw[:, E:2 * E]),
          [bc_tok], sem="v")

    with nc.named_scope("P0"), nc.Block() as blk0:
        s0.emit_into(blk0)
    st0.close()

    xt_sb = mksb(stA, "xt_sb", [128, 2 * KC * TSC], BF16)
    e0_sb = mksb(stA, "e0_sb", [128, 2 * 512], BF16)
    rr_sb = mksb(stA, "rr_sb", [128, 2 * 512], BF16)
    qt_out = mksb(stA, "qt_out", [128, 2 * SD * 512], BF16)
    kt_out = mksb(stA, "kt_out", [128, 2 * SD * 512], BF16)
    vw_out = mksb(stA, "vw_out", [128, 2 * NTB * E], BF16)
    pa = mkps(stA, "pa", [128, 2048])
    pv = mkps(stA, "pv", [128, 2048])

    sA = Sched(nc, "pA"); SCHEDS.append(sA)
    xt_tok = [None, None]
    xt_free = [None, None]
    st_tok = {"q": [None, None], "k": [None, None], "v": [None, None]}
    bank_free = [None] * 4
    reg_free = [None, None]
    phi_hist = [None, None]

    for sc in range(NSC):
        slot = sc % 2
        xin = xT_h.ap()[:, sc * TSC:(sc + 1) * TSC]
        xt_tok[slot] = sA.dma(
            "gpsimd",
            xt_sb[:, slot * KC * TSC:(slot + 1) * KC * TSC].rearrange(
                "p (k t) -> p k t", k=KC),
            xin.rearrange("(k p) t -> p k t", p=128),
            deps=[xt_free[slot]], sem="dg")
        xslot = slot * KC * TSC

        last_phi = {"q": None, "k": None}
        tile_idx = 0
        for proj, w_sb, b_sb, o_sb in (("q", wq_sb, bq_sb, qt_out),
                                       ("k", wk_sb, bk_sb, kt_out)):
            for s in range(SD):
                bank = tile_idx % 4
                mm_tok = None
                for k in range(KC):
                    lhs = w_sb[:, k * D + s * 128:k * D + (s + 1) * 128]
                    rhs = xt_sb[:, xslot + k * TSC:xslot + (k + 1) * TSC]
                    outp = pa[:, bank * 512:(bank + 1) * 512]
                    deps = []
                    if k == 0:
                        deps = [xt_tok[slot], bank_free[bank]]
                    def em(e, o=outp, l=lhs, r=rhs, st=(k == 0), sp=(k == KC - 1)):
                        return e.matmul(o, l, r, start=st, stop=sp)
                    mm_tok = sA.op("tensor", em, deps,
                                   sem="t" if k == KC - 1 else None)
                es = tile_idx % 2
                psl = pa[:, bank * 512:(bank + 1) * 512]
                def eme(e, o=e0_sb[:, es * 512:(es + 1) * 512], i=psl,
                        b=b_sb[:, s:s + 1]):
                    return e.activation(o, i, ACT.Exp, bias=b, scale=1.0)
                e0_tok = sA.op("scalar", eme, [mm_tok, phi_hist[es]], sem="s")
                def emr(e, o=rr_sb[:, es * 512:(es + 1) * 512], i=psl,
                        b=b_sb[:, s:s + 1]):
                    return e.tensor_scalar(o, i, b, 0.0, ALU.add, ALU.max)
                r_tok = sA.op("vector", emr, [e0_tok], sem="v")
                bank_free[bank] = r_tok
                def emp(e, o=o_sb[:, slot * SD * 512 + s * 512:
                                  slot * SD * 512 + (s + 1) * 512],
                        a=e0_sb[:, es * 512:(es + 1) * 512],
                        b=rr_sb[:, es * 512:(es + 1) * 512]):
                    return e.scalar_tensor_tensor(o, a, 1.0, b, ALU.min, ALU.add)
                phi_tok = sA.op("vector", emp,
                                [e0_tok, r_tok, st_tok[proj][slot]], sem="v")
                phi_hist[es] = phi_tok
                last_phi[proj] = phi_tok
                tile_idx += 1

        last_vw = None
        vmm_last = None
        for tb in range(NTB):
            reg = (tb % 2) * E
            mm_tok = None
            for k in range(KC):
                for h in range(2):
                    lhs = xt_sb[:, xslot + k * TSC + tb * 128:
                                xslot + k * TSC + (tb + 1) * 128]
                    rhs = wvwo_sb[:, k * E + h * 512:k * E + (h + 1) * 512]
                    outp = pv[:, reg + h * 512:reg + (h + 1) * 512]
                    deps = []
                    if k == 0 and h == 0:
                        deps = [xt_tok[slot], reg_free[tb % 2]]
                    def em(e, o=outp, l=lhs, r=rhs, st=(k == 0), sp=(k == KC - 1)):
                        return e.matmul(o, l, r, start=st, stop=sp)
                    mm_tok = sA.op("tensor", em, deps,
                                   sem="t" if (k == KC - 1 and h == 1) else None)
            vmm_last = mm_tok
            def emv(e, o=vw_out[:, slot * NTB * E + tb * E:
                               slot * NTB * E + (tb + 1) * E],
                    i=pv[:, reg:reg + E]):
                return e.tensor_tensor(o, i, bvWo_rep[:], ALU.add)
            last_vw = sA.op("vector", emv, [mm_tok, st_tok["v"][slot]], sem="v")
            reg_free[tb % 2] = last_vw
        xt_free[slot] = vmm_last

        st_tok["q"][slot] = sA.dma(
            "sync",
            QTd.ap()[sc * D:(sc + 1) * D, :].rearrange("(s p) t -> p s t", p=128),
            qt_out[:, slot * SD * 512:(slot + 1) * SD * 512].rearrange(
                "p (s t) -> p s t", s=SD),
            deps=[last_phi["q"]], sem="ds")
        st_tok["k"][slot] = sA.dma(
            "sync",
            KTd.ap()[sc * D:(sc + 1) * D, :].rearrange("(s p) t -> p s t", p=128),
            kt_out[:, slot * SD * 512:(slot + 1) * SD * 512].rearrange(
                "p (s t) -> p s t", s=SD),
            deps=[last_phi["k"]], sem="ds")
        st_tok["v"][slot] = sA.dma(
            "sync",
            VWd.ap()[sc * TSC:(sc + 1) * TSC, :].rearrange(
                "(tb p) e -> p tb e", p=128),
            vw_out[:, slot * NTB * E:(slot + 1) * NTB * E].rearrange(
                "p (tb e) -> p tb e", tb=NTB),
            deps=[last_vw], sem="ds")

    sA.op("sync", None, deps=[st_tok["q"][0], st_tok["q"][1],
                              st_tok["k"][0], st_tok["k"][1],
                              st_tok["v"][0], st_tok["v"][1]])
    with nc.named_scope("PA"), nc.Block() as blkA:
        sA.emit_into(blkA)
    stA.close()
    stP.close()

    sp_sb = mksb(stB, "sp_sb", [128, SD * E], F32)
    sh_sb = mksb(stB, "sh_sb", [128, SD * E], BF16)
    z_sb = mksb(stB, "z_sb", [128, SD], F32)
    zb_sb = mksb(stB, "zb_sb", [128, SD], BF16)
    kt_rb = mksb(stB, "kt_rb", [128, 4 * D], BF16)
    ko_rb = mksb(stB, "ko_rb", [128, 6 * D], BF16)
    vw_rb = mksb(stB, "vw_rb", [128, 4 * E], BF16)
    qt_rb = mksb(stB, "qt_rb", [128, 4 * D], BF16)
    x_rb = mksb(stB, "x_rb", [128, 4 * C], F32)
    xbo_sb = mksb(stB, "xbo_sb", [128, 2 * C], F32)
    r1_sb = mksb(stB, "r1_sb", [128, NTB * C], F32)
    scm_sb = mksb(stB, "scm_sb", [128, 4 * 128], BF16)
    triu_sb = mksb(stB, "triu_sb", [128, 128], F32)
    bo_sb = mksb(stB, "bo_sb", [128, C], F32)
    onec_sb = mksb(stB, "onec_sb", [128, 1], BF16)
    st1_sb = mksb(stB, "st1_sb", [128, 32], F32)
    pn0 = mkps(stB, "pn0", [128, 1536])
    pn1 = mkps(stB, "pn1", [128, 1536])
    pst = mkps(stB, "pst", [128, 1024])

    SUM, SSQ, MNEG, VPE, SR, RSTD, DENT, DENR = (0, 4, 8, 12, 16, 20, 24, 28)

    sB = Sched(nc, "pB"); SCHEDS.append(sB)
    t_triu = sB.dma("sync", triu_sb[:], triu_h.ap(), sem="ds")
    t_bo = sB.dma("sync", bo_sb[:], bo_h.ap(), sem="ds")
    t_one = sB.op("vector", lambda e: e.memset(onec_sb[:], 1.0), sem="v")
    sB.op("scalar", lambda e: (e.preload_act_table(ACT.Sqrt), e.drain())[-1])

    kt_tok = [None] * 4
    kt_free = [None] * 4
    ko_tok = [None] * 6
    ko_free = [None] * 6
    vw_tok = [None] * 4
    vw_free = [None] * 4
    qt_tok = [None] * 4
    qt_free = [None] * 4
    x_tok = [None] * 4
    x_free = [None] * 4
    sh_tok = None
    zb_tok = None
    sc_free = [None, None]
    st_half_free = [None, None]
    nreg_free = [None, None]
    y_store_tok = None
    ybf_store_tok = None
    prev_num_tok = None
    prev_sq = [None] * NTB
    sq_slot_hist = [None, None]
    cast_toks = [None] * SD
    cast_prev = [None] * SD

    def g_of(sc, tb):
        return sc * NTB + tb

    for sc in range(NSC):
        for tb in range(NTB):
            g = g_of(sc, tb)
            k4, q3 = g % 4, g % 4
            x6 = g % 4
            qt_tok[q3] = sB.dma(
                "sync",
                qt_rb[:, q3 * D:(q3 + 1) * D].rearrange("p (s t) -> p s t", s=SD),
                QTd.ap()[sc * D:(sc + 1) * D,
                         tb * 128:(tb + 1) * 128].rearrange(
                             "(s p) t -> p s t", p=128),
                deps=[qt_free[q3]], sem="ds")
            kt_tok[k4] = sB.dma(
                "sync",
                kt_rb[:, k4 * D:(k4 + 1) * D].rearrange("p (s t) -> p s t", s=SD),
                KTd.ap()[sc * D:(sc + 1) * D,
                         tb * 128:(tb + 1) * 128].rearrange(
                             "(s p) t -> p s t", p=128),
                deps=[kt_free[k4]], sem="ds")
            x_tok[x6] = sB.dma(
                "gpsimd", x_rb[:, x6 * C:(x6 + 1) * C],
                x_h.ap()[g * 128:(g + 1) * 128, :],
                deps=[x_free[x6]], sem="dg")
            vw_tok[k4] = sB.dma(
                "gpsimd", vw_rb[:, k4 * E:(k4 + 1) * E],
                VWd.ap()[g * 128:(g + 1) * 128, :],
                deps=[vw_free[k4]], sem="dg")
        for tb in range(NTB):
            g = g_of(sc, tb)
            k8 = g % 6
            ko_tok[k8] = sB.dma(
                "sync", ko_rb[:, k8 * D:(k8 + 1) * D],
                KTd.ap()[sc * D:(sc + 1) * D, tb * 128:(tb + 1) * 128],
                deps=[ko_free[k8]], sem="ds", transpose=True)

        norm_toks = [None] * NTB
        sq_toks = [None] * NTB
        pn0_free_for_z = None
        sc_evac = []
        evac_j_last = [None] * NTB
        last_num_tok = None
        for tb in range(NTB):
            g = g_of(sc, tb)
            q3 = g % 4
            x6 = g % 4
            nreg = pn0 if tb % 2 == 0 else pn1
            qsl = qt_rb[:, q3 * D:(q3 + 1) * D]

            xb = tb % 2
            def emx(e, o=xbo_sb[:, xb * C:(xb + 1) * C],
                    i=x_rb[:, x6 * C:(x6 + 1) * C]):
                return e.tensor_tensor(o, i, bo_sb[:], ALU.add)
            xbo_tok = sB.op("vector", emx, [x_tok[x6], t_bo, sq_slot_hist[xb]], sem="v")
            x_free[x6] = xbo_tok

            sc_evac = []
            for j in range(tb + 1):
                j5 = g_of(sc, j) % 4
                par = j % 2
                mm_tok = None
                for k in range(SD):
                    lhs = kt_rb[:, j5 * D + k * 128:j5 * D + (k + 1) * 128]
                    rhs = qsl[:, k * 128:(k + 1) * 128]
                    outp = pst[:, par * 512:par * 512 + 128]
                    deps = []
                    if k == 0:
                        deps = [kt_tok[j5], qt_tok[q3], sc_free[par]]
                    def em(e, o=outp, l=lhs, r=rhs, st=(k == 0), sp=(k == SD - 1)):
                        return e.matmul(o, l, r, start=st, stop=sp)
                    mm_tok = sB.op("tensor", em, deps,
                                   sem="t" if k == SD - 1 else None)
                if j == tb:
                    def eme(e, o=scm_sb[:, j * 128:(j + 1) * 128],
                            i=pst[:, par * 512:par * 512 + 128]):
                        return e.tensor_tensor(o, i, triu_sb[:], ALU.mult)
                    ev_tok = sB.op("vector", eme, [mm_tok, t_triu, prev_num_tok],
                                   sem="v")
                else:
                    def eme(e, o=scm_sb[:, j * 128:(j + 1) * 128],
                            i=pst[:, par * 512:par * 512 + 128]):
                        return e.activation(o, i, ACT.Copy)
                    ev_tok = sB.op("scalar", eme, [mm_tok, prev_num_tok],
                                   sem="s")
                sc_free[par] = ev_tok
                sc_evac.append(ev_tok)
                evac_j_last[j] = ev_tok

            started = [False, False, False]
            nmm_tok = None

            def nm(rgn, lhs, rhs, extra=(), last=False):
                nonlocal nmm_tok
                lo = (0, 512, 1024)[rgn]
                hi = (512, 1024, 1025)[rgn]
                st = not started[rgn]
                started[rgn] = True
                deps = [d for d in extra if d is not None]
                def em(e, o=nreg[:, lo:hi], l=lhs, r=rhs, st=st, sp=last):
                    return e.matmul(o, l, r, start=st, stop=sp)
                nmm_tok = sB.op("tensor", em, deps, sem="t" if last else None)

            only_intra = (sc == 0)
            for j in range(tb + 1):
                j5 = g_of(sc, j) % 4
                lhs = scm_sb[:, j * 128:(j + 1) * 128]
                isl = (j == tb) and only_intra
                nm(0, lhs, vw_rb[:, j5 * E:j5 * E + 512],
                   extra=[sc_evac[j], vw_tok[j5],
                          nreg_free[tb % 2] if j == 0 else None], last=isl)
                nm(1, lhs, vw_rb[:, j5 * E + 512:(j5 + 1) * E], last=isl)
                nm(2, lhs, onec_sb[:], extra=[t_one], last=isl)
            if sc > 0:
                for s in range(SD):
                    lhs = qsl[:, s * 128:(s + 1) * 128]
                    lasts = (s == SD - 1)
                    nm(0, lhs, sh_sb[:, s * E:s * E + 512],
                       extra=[cast_prev[s]], last=lasts)
                    nm(1, lhs, sh_sb[:, s * E + 512:(s + 1) * E], last=lasts)
                    nm(2, lhs, zb_sb[:, s:s + 1],
                       extra=[zb_tok] if s == 0 else (), last=lasts)
            num_tok = nmm_tok
            last_num_tok = num_tok
            prev_num_tok = num_tok

            def emd(e, o=st1_sb[:, DENT + tb:DENT + tb + 1],
                    i=nreg[:, 1024:1025]):
                return e.tensor_scalar(o, i, 1e-6, None, ALU.add)
            dent_tok = sB.op("vector", emd, [num_tok], otok=True)
            def emr2(e, o=st1_sb[:, DENR + tb:DENR + tb + 1],
                     i=st1_sb[:, DENT + tb:DENT + tb + 1]):
                return e.reciprocal(o, i)
            denr_tok = sB.op("vector", emr2, [dent_tok], otok=True)
            def em1(e, o=r1_sb[:, tb * C:(tb + 1) * C], i=nreg[:, 0:1024],
                    sc1=st1_sb[:, DENR + tb:DENR + tb + 1],
                    i2=xbo_sb[:, xb * C:(xb + 1) * C],
                    ac=st1_sb[:, SUM + tb:SUM + tb + 1]):
                return e.scalar_tensor_tensor(o, i, sc1, i2, ALU.mult, ALU.add,
                                              accum_out=ac)
            r1_tok = sB.op("vector", em1,
                           [denr_tok, xbo_tok, prev_sq[tb],
                            y_store_tok, ybf_store_tok], sem="v")
            nreg_free[tb % 2] = r1_tok
            if tb == 2:
                pn0_free_for_z = r1_tok
            def ems(e, o=xbo_sb[:, xb * C:(xb + 1) * C],
                    i=r1_sb[:, tb * C:(tb + 1) * C],
                    ac=st1_sb[:, SSQ + tb:SSQ + tb + 1]):
                return e.activation(o, i, ACT.Square, accum_out=ac)
            sq_toks[tb] = sB.op("scalar", ems, [r1_tok], sem="s")
            prev_sq[tb] = sq_toks[tb]
            sq_slot_hist[xb] = sq_toks[tb]

        if sc < NSC - 1:
            for s in range(SD):
                half_tok = [None, None]
                smm_tok = None
                for h in range(2):
                    mm_tok = None
                    for tb in range(NTB):
                        k5 = g_of(sc, tb) % 6
                        k4 = g_of(sc, tb) % 4
                        lhs = ko_rb[:, k5 * D + s * 128:k5 * D + (s + 1) * 128]
                        rhs = vw_rb[:, k4 * E + h * 512:k4 * E + (h + 1) * 512]
                        outp = pst[:, h * 512:(h + 1) * 512]
                        deps = [ko_tok[g_of(sc, tb) % 6], vw_tok[k4]]
                        if tb == 0:
                            deps.append(st_half_free[h])
                            if s == 0 and h == 0:
                                deps += [sc_free[0], sc_free[1]]
                        def em(e, o=outp, l=lhs, r=rhs, st=(tb == 0),
                               sp=(tb == NTB - 1)):
                            return e.matmul(o, l, r, start=st, stop=sp)
                        mm_tok = sB.op("tensor", em, deps,
                                       sem="t" if tb == NTB - 1 else None)
                    smm_tok = mm_tok
                    lo = s * E + h * 512
                    if sc == 0:
                        def eme2(e, o=sp_sb[:, lo:lo + 512],
                                 i=pst[:, h * 512:(h + 1) * 512]):
                            return e.tensor_copy(o, i)
                    else:
                        def eme2(e, o=sp_sb[:, lo:lo + 512],
                                 i=pst[:, h * 512:(h + 1) * 512],
                                 b=sp_sb[:, lo:lo + 512]):
                            return e.tensor_tensor(o, i, b, ALU.add)
                    half_tok[h] = sB.op("vector", eme2,
                                        [mm_tok, cast_toks[s]], sem="v")
                    st_half_free[h] = half_tok[h]
                def emc(e, o=sh_sb[:, s * E:(s + 1) * E],
                        i=sp_sb[:, s * E:(s + 1) * E]):
                    return e.activation(o, i, ACT.Copy)
                cast_toks[s] = sB.op("scalar", emc,
                                     [half_tok[1], last_num_tok], sem="s")
            sh_tok = cast_toks[SD - 1]
            cast_prev = list(cast_toks)
            sc_free = [st_half_free[0], st_half_free[1]]

            zmm_tok = None
            for s in range(SD):
                for tb in range(NTB):
                    k5 = g_of(sc, tb) % 6
                    lhs = ko_rb[:, k5 * D + s * 128:k5 * D + (s + 1) * 128]
                    outp = pn0[:, 1040 + s:1041 + s]
                    deps = [ko_tok[g_of(sc, tb) % 6]]
                    if s == 0 and tb == 0:
                        deps += [pn0_free_for_z, t_one]
                    def em(e, o=outp, l=lhs, r=onec_sb[:], st=(tb == 0),
                           sp=(tb == NTB - 1)):
                        return e.matmul(o, l, r, start=st, stop=sp)
                    zmm_tok = sB.op("tensor", em, deps,
                                    sem="t" if (s == SD - 1 and tb == NTB - 1)
                                    else None)
            if sc == 0:
                def emz(e, o=z_sb[:], i=pn0[:, 1040:1040 + SD]):
                    return e.tensor_copy(o, i)
            else:
                def emz(e, o=z_sb[:], i=pn0[:, 1040:1040 + SD], b=z_sb[:]):
                    return e.tensor_tensor(o, i, b, ALU.add)
            zev_tok = sB.op("vector", emz, [zmm_tok, zb_tok], sem="v")
            nreg_free[0] = zev_tok
            def emzc(e, o=zb_sb[:], i=z_sb[:]):
                return e.activation(o, i, ACT.Copy)
            zb_tok = sB.op("scalar", emzc, [zev_tok, last_num_tok], sem="s")

        def emm(e, o=st1_sb[:, MNEG:MNEG + 4], i=st1_sb[:, SUM:SUM + 4]):
            return e.tensor_scalar(o, i, -1.0 / C, None, ALU.mult)
        mneg_tok = sB.op("vector", emm, [sq_toks[-1]], otok=True)
        def emv1(e, o=st1_sb[:, VPE:VPE + 4], i=st1_sb[:, SSQ:SSQ + 4]):
            return e.tensor_scalar(o, i, 1.0 / C, 1e-5, ALU.mult, ALU.add)
        vpe_tok = sB.op("vector", emv1, [sq_toks[-1]], otok=True)
        def emv2(e, o=st1_sb[:, SR:SR + 4], a=st1_sb[:, MNEG:MNEG + 4]):
            return e.tensor_tensor(o, a, a, ALU.mult)
        msq_tok = sB.op("vector", emv2, [mneg_tok], otok=True)
        def emv3(e, o=st1_sb[:, VPE:VPE + 4], a=st1_sb[:, VPE:VPE + 4],
                 b=st1_sb[:, SR:SR + 4]):
            return e.tensor_tensor(o, a, b, ALU.subtract)
        var_tok = sB.op("vector", emv3, [msq_tok, vpe_tok], sem="v")
        def emsq(e, o=st1_sb[:, SR:SR + 4], i=st1_sb[:, VPE:VPE + 4]):
            return e.activation(o, i, ACT.Sqrt)
        sr_tok = sB.op("scalar", emsq, [var_tok], sem="s")
        def emrc(e, o=st1_sb[:, RSTD:RSTD + 4], i=st1_sb[:, SR:SR + 4]):
            return e.reciprocal(o, i)
        rstd_tok = sB.op("vector", emrc, [sr_tok], otok=True)
        for tb in range(NTB):
            def emn(e, o=r1_sb[:, tb * C:(tb + 1) * C],
                    i=r1_sb[:, tb * C:(tb + 1) * C],
                    m=st1_sb[:, MNEG + tb:MNEG + tb + 1],
                    r=st1_sb[:, RSTD + tb:RSTD + tb + 1]):
                return e.tensor_scalar(o, i, m, r, ALU.add, ALU.mult)
            norm_toks[tb] = sB.op("vector", emn, [rstd_tok], sem="v")

        y_store_tok = sB.dma(
            "gpsimd",
            Yf.ap()[sc * TSC:(sc + 1) * TSC, :].rearrange(
                "(tb p) c -> p tb c", p=128),
            r1_sb[:].rearrange("p (tb c) -> p tb c", tb=NTB),
            deps=[norm_toks[3]], sem="dg")
        ybf_store_tok = sB.dma(
            "gpsimd",
            Ybf.ap()[sc * TSC:(sc + 1) * TSC, :].rearrange(
                "(tb p) c -> p tb c", p=128),
            r1_sb[:].rearrange("p (tb c) -> p tb c", tb=NTB),
            deps=[norm_toks[3]], sem="dg")

        sB.dma("gpsimd", dST1d.ap()[sc * 128:(sc + 1) * 128, :], st1_sb[:],
               deps=[norm_toks[3]], sem="dg")

        if sc < NSC - 1:
            for tb in range(NTB):
                g = g_of(sc, tb)
                ko_free[g % 6] = zmm_tok
                vw_free[g % 4] = zmm_tok
                qt_free[g % 4] = norm_toks[tb]
                kt_free[g % 4] = evac_j_last[tb]
        else:
            for tb in range(NTB):
                qt_free[g_of(sc, tb) % 4] = norm_toks[tb]

    sB.op("gpsimd", None, deps=[y_store_tok, ybf_store_tok])
    with nc.named_scope("PB"), nc.Block() as blkB:
        sB.emit_into(blkB)
    stB.close()

    wr_sb = mksb(stC, "wr_sb", [128, KC * C], BF16)
    wkt_sb = mksb(stC, "wkt_sb", [128, KC * C], BF16)
    wvf_sb = mksb(stC, "wvf_sb", [128, KC * C], BF16)
    mur_sb = mksb(stC, "mur_sb", [128, KC], F32)
    muk_sb = mksb(stC, "muk_sb", [128, KC], F32)
    omr_sb = mksb(stC, "omr_sb", [128, KC], F32)
    omk_sb = mksb(stC, "omk_sb", [128, KC], F32)
    yt_sb = mksb(stC, "yt_sb", [128, 3 * KC * 512], BF16)
    xr_sb = mksb(stC, "xr_sb", [128, 2 * KC * 512], BF16)
    xk_sb = mksb(stC, "xk_sb", [128, 2 * KC * 512], BF16)
    tmx_sb = mksb(stC, "tmx_sb", [128, 2 * 512], F32)
    kk_sb = mksb(stC, "kk_sb", [128, 2 * KC * 512], BF16)
    zc_sb = mksb(stC, "zc_sb", [128, 1], BF16)
    krl_sb = mksb(stC, "krl_sb", [128, 2 * 512], BF16)
    rs_sb = mksb(stC, "rs_sb", [128, 2 * C], F32)
    yre_sb = mksb(stC, "yre_sb", [128, 2 * C], F32)
    f_sb = mksb(stC, "f_sb", [128, C], F32)
    r2_sb = mksb(stC, "r2_sb", [128, NTB * C], F32)
    sq2_sb = mksb(stC, "sq2_sb", [128, C], F32)
    st2_sb = mksb(stC, "st2_sb", [128, 32], F32)
    pk = mkps(stC, "pk", [128, 1024])
    pr = mkps(stC, "pr", [128, 1024])
    pvv = mkps(stC, "pvv", [128, 2048])

    SUM2, SSQ2, MNEG2, VPE2, SR2, RSTD2 = (0, 4, 8, 12, 16, 20)

    sC = Sched(nc, "pC"); SCHEDS.append(sC)
    t_wr = sC.dma("gpsimd",
                  wr_sb[:].rearrange("p (k d) -> p k d", k=KC),
                  WrT_h.ap().rearrange("(k p) d -> p k d", p=128), sem="dg")
    t_wkt = sC.dma("gpsimd",
                   wkt_sb[:].rearrange("p (k d) -> p k d", k=KC),
                   WkTf_h.ap().rearrange("(k p) d -> p k d", p=128), sem="dg")
    t_wvf = sC.dma("gpsimd",
                   wvf_sb[:].rearrange("p (k d) -> p k d", k=KC),
                   WvTf_h.ap().rearrange("(k p) d -> p k d", p=128), sem="dg")
    t_mur = sC.dma("sync", mur_sb[:], mur_h.ap(), sem="ds")
    t_muk = sC.dma("sync", muk_sb[:], muk_h.ap(), sem="ds")
    def emo1(e):
        return e.tensor_scalar(omr_sb[:], mur_sb[:], -1.0, 1.0, ALU.mult, ALU.add)
    t_omr = sC.op("vector", emo1, [t_mur], sem="v")
    def emo2(e):
        return e.tensor_scalar(omk_sb[:], muk_sb[:], -1.0, 1.0, ALU.mult, ALU.add)
    t_omk = sC.op("vector", emo2, [t_muk], sem="v")
    sC.op("scalar", lambda e: (e.preload_act_table(ACT.Sigmoid), e.drain())[-1])

    t_zc = sC.op("vector", lambda e: e.memset(zc_sb[:], 0.0), sem="v")
    yt_tok = [None] * 3
    yt_free = [None] * 3
    xrk_free = [None] * 2
    kkb_free = [None] * 2
    yre_tok = [None] * 2
    yre_free = [None] * 2
    o_free = [None] * 2
    pk_free = [None, None]
    out_prev_store = None
    pr_free = None
    pv_free = [None, None]
    f_hist = [None, None]
    prev_sq2 = [None] * NTB
    final_store = None

    def emit_yt_loads(sc):
        slot = sc % 3
        ytt = None
        for si in range(KC):
            ytt = sC.dma("scalar",
                         yt_sb[:, slot * KC * 512 + si * 512:
                               slot * KC * 512 + (si + 1) * 512],
                         Ybf.ap()[sc * TSC:(sc + 1) * TSC,
                                  si * 128:(si + 1) * 128],
                         deps=[yt_free[slot]] if si == 0 else (),
                         sem="ds2", transpose=True)
        yt_tok[slot] = ytt

    def emit_mix(sc):
        slot = sc % 3
        pslot = (sc - 1) % 3
        mix_last = {"r": None, "k": None}
        xof = (sc % 2) * KC * 512
        for si in range(KC):
            ysl = yt_sb[:, slot * KC * 512 + si * 512:slot * KC * 512 + (si + 1) * 512]
            psh = yt_sb[:, slot * KC * 512 + si * 512:slot * KC * 512 + si * 512 + 511]
            if sc == 0:
                pcol = zc_sb[:]
                pcol_dep = t_zc
            else:
                pcol = yt_sb[:, pslot * KC * 512 + si * 512 + 511:
                             pslot * KC * 512 + si * 512 + 512]
                pcol_dep = yt_tok[pslot]
            for which, xo_sb, mu, om in (("r", xr_sb, mur_sb, omr_sb),
                                         ("k", xk_sb, muk_sb, omk_sb)):
                ts_ = (si % 2)
                def emt(e, o=tmx_sb[:, ts_ * 512 + 1:(ts_ + 1) * 512], i=psh,
                        m=om[:, si:si + 1]):
                    return e.tensor_scalar(o, i, m, None, ALU.mult)
                tm_tok = sC.op("vector", emt,
                               [yt_tok[slot], t_omr, t_omk, xrk_free[sc % 2]],
                               otok=True)
                def emt0(e, o=tmx_sb[:, ts_ * 512:ts_ * 512 + 1], i=pcol,
                         m=om[:, si:si + 1]):
                    return e.tensor_scalar(o, i, m, None, ALU.mult)
                tm0_tok = sC.op("vector", emt0, [pcol_dep], otok=True)
                def emm2(e, o=xo_sb[:, xof + si * 512:xof + (si + 1) * 512], i=ysl,
                         m=mu[:, si:si + 1],
                         b=tmx_sb[:, ts_ * 512:(ts_ + 1) * 512]):
                    return e.scalar_tensor_tensor(o, i, m, b, ALU.mult, ALU.add)
                mix_last[which] = sC.op("vector", emm2,
                                        [tm_tok, tm0_tok, t_mur, t_muk],
                                        sem="v")
        yt_free[pslot] = mix_last["k"]
        return mix_last

    emit_yt_loads(0)
    emit_yt_loads(1)
    mixes = {0: emit_mix(0)}
    for sc in range(NSC):
        slot = sc % 2
        xof = slot * KC * 512
        if sc + 2 < NSC:
            emit_yt_loads(sc + 2)
        if sc + 1 < NSC:
            mixes[sc + 1] = emit_mix(sc + 1)
        mix_last = mixes[sc]

        kk_last = None
        for s in range(KC):
            par = s % 2
            mm_tok = None
            for k in range(KC):
                lhs = wkt_sb[:, k * C + s * 128:k * C + (s + 1) * 128]
                rhs = xk_sb[:, xof + k * 512:xof + (k + 1) * 512]
                outp = pk[:, par * 512:(par + 1) * 512]
                deps = []
                if k == 0:
                    deps = [mix_last["k"], t_wkt, pk_free[par]]
                def em(e, o=outp, l=lhs, r=rhs, st=(k == 0), sp=(k == KC - 1)):
                    return e.matmul(o, l, r, start=st, stop=sp)
                mm_tok = sC.op("tensor", em, deps,
                               sem="t" if k == KC - 1 else None)
            def emrl(e, o=krl_sb[:, par * 512:(par + 1) * 512],
                     i=pk[:, par * 512:(par + 1) * 512]):
                return e.activation(o, i, ACT.Relu)
            rl_tok = sC.op("scalar", emrl, [mm_tok], sem="s")
            pk_free[par] = rl_tok
            def emsq2(e, o=kk_sb[:, xof + s * 512:xof + (s + 1) * 512],
                      i=krl_sb[:, par * 512:(par + 1) * 512]):
                return e.activation(o, i, ACT.Square)
            kk_last = sC.op("scalar", emsq2, [rl_tok, kkb_free[slot]], sem="s")

        sq2_toks = [None] * NTB
        last_v_mm = None
        for tb in range(NTB):
            g = g_of(sc, tb)
            yr2 = tb % 2
            yre_tok[yr2] = sC.dma("gpsimd",
                                  yre_sb[:, yr2 * C:(yr2 + 1) * C],
                                  Yf.ap()[g * 128:(g + 1) * 128, :],
                                  deps=[yre_free[yr2]], sem="dg")
            mm_tok = None
            for k in range(KC):
                for h in range(2):
                    lhs = xr_sb[:, xof + k * 512 + tb * 128:xof + k * 512 + (tb + 1) * 128]
                    rhs = wr_sb[:, k * C + h * 512:k * C + (h + 1) * 512]
                    outp = pr[:, h * 512:(h + 1) * 512]
                    deps = []
                    if k == 0 and h == 0:
                        deps = [mix_last["r"], t_wr, pr_free]
                    def em(e, o=outp, l=lhs, r=rhs, st=(k == 0), sp=(k == KC - 1)):
                        return e.matmul(o, l, r, start=st, stop=sp)
                    mm_tok = sC.op("tensor", em, deps,
                                   sem="t" if (k == KC - 1 and h == 1) else None)
            rsl = tb % 2
            def emsg(e, o=rs_sb[:, rsl * C:(rsl + 1) * C], i=pr[:, 0:1024]):
                return e.activation(o, i, ACT.Sigmoid)
            sg_tok = sC.op("scalar", emsg, [mm_tok, f_hist[rsl]], sem="s")
            pr_free = sg_tok
            vreg = (tb % 2) * E
            mm_tok = None
            for k in range(KC):
                for h in range(2):
                    lhs = kk_sb[:, xof + k * 512 + tb * 128:xof + k * 512 + (tb + 1) * 128]
                    rhs = wvf_sb[:, k * C + h * 512:k * C + (h + 1) * 512]
                    outp = pvv[:, vreg + h * 512:vreg + (h + 1) * 512]
                    deps = []
                    if k == 0 and h == 0:
                        deps = [kk_last, t_wvf, pv_free[tb % 2]]
                    def em(e, o=outp, l=lhs, r=rhs, st=(k == 0), sp=(k == KC - 1)):
                        return e.matmul(o, l, r, start=st, stop=sp)
                    mm_tok = sC.op("tensor", em, deps,
                                   sem="t" if (k == KC - 1 and h == 1) else None)
            last_v_mm = mm_tok
            def emf(e, o=f_sb[:], i=pvv[:, vreg:vreg + E],
                    b=rs_sb[:, rsl * C:(rsl + 1) * C]):
                return e.tensor_tensor(o, i, b, ALU.mult)
            f_tok = sC.op("vector", emf, [mm_tok, sg_tok], sem="v")
            pv_free[tb % 2] = f_tok
            f_hist[rsl] = f_tok
            def em2(e, o=r2_sb[:, tb * C:(tb + 1) * C], i=f_sb[:],
                    b=yre_sb[:, yr2 * C:(yr2 + 1) * C],
                    ac=st2_sb[:, SUM2 + tb:SUM2 + tb + 1]):
                return e.scalar_tensor_tensor(o, i, 1.0, b, ALU.mult, ALU.add,
                                              accum_out=ac)
            r2_tok = sC.op("vector", em2,
                           [f_tok, yre_tok[yr2], prev_sq2[tb], out_prev_store],
                           sem="v")
            yre_free[yr2] = r2_tok
            def ems2(e, o=sq2_sb[:], i=r2_sb[:, tb * C:(tb + 1) * C],
                     ac=st2_sb[:, SSQ2 + tb:SSQ2 + tb + 1]):
                return e.activation(o, i, ACT.Square, accum_out=ac)
            sq2_toks[tb] = sC.op("scalar", ems2, [r2_tok], sem="s")
            prev_sq2[tb] = sq2_toks[tb]
        xrk_free[slot] = last_v_mm
        kkb_free[slot] = last_v_mm

        def emm_(e, o=st2_sb[:, MNEG2:MNEG2 + 4], i=st2_sb[:, SUM2:SUM2 + 4]):
            return e.tensor_scalar(o, i, -1.0 / C, None, ALU.mult)
        mneg_tok = sC.op("vector", emm_, [sq2_toks[-1]], otok=True)
        def emv1_(e, o=st2_sb[:, VPE2:VPE2 + 4], i=st2_sb[:, SSQ2:SSQ2 + 4]):
            return e.tensor_scalar(o, i, 1.0 / C, 1e-5, ALU.mult, ALU.add)
        vpe_tok = sC.op("vector", emv1_, [sq2_toks[-1]], otok=True)
        def emv2_(e, o=st2_sb[:, SR2:SR2 + 4], a=st2_sb[:, MNEG2:MNEG2 + 4]):
            return e.tensor_tensor(o, a, a, ALU.mult)
        msq_tok = sC.op("vector", emv2_, [mneg_tok], otok=True)
        def emv3_(e, o=st2_sb[:, VPE2:VPE2 + 4], a=st2_sb[:, VPE2:VPE2 + 4],
                  b=st2_sb[:, SR2:SR2 + 4]):
            return e.tensor_tensor(o, a, b, ALU.subtract)
        var_tok = sC.op("vector", emv3_, [msq_tok, vpe_tok], sem="v")
        def emsd(e, o=st2_sb[:, SR2:SR2 + 4], i=st2_sb[:, VPE2:VPE2 + 4]):
            return e.tensor_scalar(o, i, 0.5, 0.5, ALU.mult, ALU.add)
        it_tok = sC.op("vector", emsd, [var_tok], otok=True)
        for _ in range(5):
            def emr1_(e, o=st2_sb[:, 24:28], i=st2_sb[:, SR2:SR2 + 4]):
                return e.reciprocal(o, i)
            t1 = sC.op("vector", emr1_, [it_tok], otok=True)
            def emr2_(e, o=st2_sb[:, 28:32], a=st2_sb[:, 24:28],
                      b=st2_sb[:, VPE2:VPE2 + 4]):
                return e.tensor_tensor(o, a, b, ALU.mult)
            t2 = sC.op("vector", emr2_, [t1], otok=True)
            def emr3_(e, o=st2_sb[:, SR2:SR2 + 4], a=st2_sb[:, SR2:SR2 + 4],
                      b=st2_sb[:, 28:32]):
                return e.tensor_tensor(o, a, b, ALU.add)
            t3 = sC.op("vector", emr3_, [t2], otok=True)
            def emr4_(e, o=st2_sb[:, SR2:SR2 + 4], i=st2_sb[:, SR2:SR2 + 4]):
                return e.tensor_scalar(o, i, 0.5, None, ALU.mult)
            it_tok = sC.op("vector", emr4_, [t3], otok=True)
        def emrc_(e, o=st2_sb[:, RSTD2:RSTD2 + 4], i=st2_sb[:, SR2:SR2 + 4]):
            return e.reciprocal(o, i)
        rstd_tok = sC.op("vector", emrc_, [it_tok], otok=True)
        o_norm = None
        for tb in range(NTB):
            def emn_(e, o=r2_sb[:, tb * C:(tb + 1) * C],
                     i=r2_sb[:, tb * C:(tb + 1) * C],
                     m=st2_sb[:, MNEG2 + tb:MNEG2 + tb + 1],
                     r=st2_sb[:, RSTD2 + tb:RSTD2 + tb + 1]):
                return e.tensor_scalar(o, i, m, r, ALU.add, ALU.mult)
            o_norm = sC.op("vector", emn_, [rstd_tok], sem="v")
        final_store = sC.dma(
            "sync",
            out_h.ap()[sc * TSC:(sc + 1) * TSC, :].rearrange(
                "(tb p) c -> p tb c", p=128),
            r2_sb[:].rearrange("p (tb c) -> p tb c", tb=NTB),
            deps=[o_norm], sem="dout")
        o_free[slot] = final_store
        out_prev_store = final_store

    sC.op("sync", None, deps=[final_store])
    sC.op("gpsimd", None, deps=[final_store])

    with nc.named_scope("PC"), nc.Block() as blkC:
        sC.emit_into(blkC)
    stC.close()

    if debug:
        dQT = nc.dram_tensor("dQT", [NSC * D, TSC], BF16, kind="ExternalOutput")
        dKT = nc.dram_tensor("dKT", [NSC * D, TSC], BF16, kind="ExternalOutput")
        dVW = nc.dram_tensor("dVW", [N, E], BF16, kind="ExternalOutput")
        dYf = nc.dram_tensor("dYf", [N, C], F32, kind="ExternalOutput")
        dYb = nc.dram_tensor("dYb", [N, C], BF16, kind="ExternalOutput")
        dST1 = nc.dram_tensor("dST1", [NSC * 128, 32], F32, kind="ExternalOutput")
        sD = Sched(nc, "pD")
        tks = [sD.dma("gpsimd", dQT.ap(), QTd.ap(), sem="dg"),
               sD.dma("gpsimd", dKT.ap(), KTd.ap(), sem="dg"),
               sD.dma("gpsimd", dVW.ap(), VWd.ap(), sem="dg"),
               sD.dma("gpsimd", dYf.ap(), Yf.ap(), sem="dg"),
               sD.dma("gpsimd", dYb.ap(), Ybf.ap(), sem="dg"),
               sD.dma("gpsimd", dST1.ap(), dST1d.ap(), sem="dg")]
        sD.op("gpsimd", None, deps=tks)
        with nc.Block() as blkD:
            sD.emit_into(blkD)

    nc.finalize()
    return nc


_nc_cache = None


def kernel(L0_attn_x, L0_attn_Wq, L0_attn_bq, L0_attn_Wk, L0_attn_bk,
           L0_attn_Wv, L0_attn_bv, L0_attn_Wo, L0_attn_bo,
           L0_ffn_W_r, L0_ffn_W_k, L0_ffn_W_v, L0_ffn_mu_r, L0_ffn_mu_k):
    global _nc_cache
    f = np.float32
    ca = np.ascontiguousarray

    x = np.asarray(L0_attn_x, f)
    shared = dict(
        Wq=ca(np.asarray(L0_attn_Wq, f)),
        Wk=ca(np.asarray(L0_attn_Wk, f)),
        WvT=ca(np.asarray(L0_attn_Wv, f).T),
        Wo=ca(np.asarray(L0_attn_Wo, f)),
        bqc=ca(np.asarray(L0_attn_bq, f).reshape(SD, 128).T),
        bkc=ca(np.asarray(L0_attn_bk, f).reshape(SD, 128).T),
        bvc=ca(np.asarray(L0_attn_bv, f).reshape(SD, 128).T),
        bo_rep=ca(np.broadcast_to(np.asarray(L0_attn_bo, f), (128, C))),
        WrT=ca(np.asarray(L0_ffn_W_r, f).T),
        WkTf=ca(np.asarray(L0_ffn_W_k, f).T),
        WvTf=ca(np.asarray(L0_ffn_W_v, f).T),
        muc_r=ca(np.asarray(L0_ffn_mu_r, f).reshape(KC, 128).T),
        muc_k=ca(np.asarray(L0_ffn_mu_k, f).reshape(KC, 128).T),
        triu=np.triu(np.ones((128, 128), f)),
        ident=np.eye(128, dtype=f),
    )
    in_maps = []
    for b in range(NCORES):
        m = dict(shared)
        m["x"] = ca(x[b])
        m["xT"] = ca(x[b].T)
        in_maps.append(m)

    if _nc_cache is None:
        _nc_cache = build_nc()
    res = run_bass_kernel_spmd(_nc_cache, in_maps, core_ids=list(range(NCORES)))
    return np.stack([r["out"] for r in res.results], axis=0)
